# version 23
# baseline (speedup 1.0000x reference)
import os
import numpy as np
import ml_dtypes
import concourse.bass as bass
import concourse.mybir as mybir
from concourse.bass_utils import run_bass_kernel_spmd

F32 = mybir.dt.float32
BF16 = mybir.dt.bfloat16
AF = mybir.ActivationFunctionType
OP = mybir.AluOpType
NPBF = ml_dtypes.bfloat16

SEQ = 2048
DM = 1024
DI = 2048
DH = 4096
NS = 16
TC = 512
NCHUNK = SEQ // TC
NDG = DM // 128
NIG = DI // 128
NHG = DH // 128

LN2 = float(np.log(2.0))
F = lambda v: float(np.float32(v))


def _threefry_bits(n):
    idx = np.arange(n, dtype=np.uint64)
    x0 = (idx >> np.uint64(32)).astype(np.uint32)
    x1 = (idx & np.uint64(0xFFFFFFFF)).astype(np.uint32)
    rot = [np.uint32(r) for r in (13, 15, 26, 6, 17, 29, 16, 24)]
    ks0 = np.uint32(0)
    ks1 = np.uint32(42)
    ks2 = np.uint32(np.uint32(0x1BD11BDA) ^ ks0 ^ ks1)

    def rotl(x, r):
        return (x << r) | (x >> (np.uint32(32) - r))

    def rounds(x0, x1, rs):
        for r in rs:
            x0 = x0 + x1
            x1 = rotl(x1, r)
            x1 = x1 ^ x0
        return x0, x1

    with np.errstate(over="ignore"):
        x0 = x0 + ks0
        x1 = x1 + ks1
        x0, x1 = rounds(x0, x1, rot[:4])
        x0 = x0 + ks1
        x1 = x1 + ks2 + np.uint32(1)
        x0, x1 = rounds(x0, x1, rot[4:])
        x0 = x0 + ks2
        x1 = x1 + ks0 + np.uint32(2)
        x0, x1 = rounds(x0, x1, rot[:4])
        x0 = x0 + ks0
        x1 = x1 + ks1 + np.uint32(3)
        x0, x1 = rounds(x0, x1, rot[4:])
        x0 = x0 + ks1
        x1 = x1 + ks2 + np.uint32(4)
        x0, x1 = rounds(x0, x1, rot[:4])
        x0 = x0 + ks2
        x1 = x1 + ks0 + np.uint32(5)
    return x0, x1


def _keep_mask(fp64_mode):
    n = 8 * SEQ * DM
    b1, b2 = _threefry_bits(n)
    if fp64_mode:
        u64 = (b1.astype(np.uint64) << np.uint64(32)) | b2.astype(np.uint64)
        fb = (u64 >> np.uint64(12)) | np.uint64(0x3FF0000000000000)
        keep = (fb.view(np.float64) - 1.0) < 0.8
    else:
        bits = b1 ^ b2
        fb = (bits >> np.uint32(9)) | np.uint32(0x3F800000)
        keep = (fb.view(np.float32) - np.float32(1.0)) < np.float32(0.8)
    return keep.reshape(8, SEQ, DM)


class Ctr:

    def __init__(self, nc):
        self.sem = {}
        self.cnt = {}
        for name in ("pe", "act", "dve", "pool", "ld", "st"):
            self.sem[name] = nc.alloc_semaphore("sem_" + name)
            self.cnt[name] = 0

    def inc(self, inst, which, n=1):
        inst.then_inc(self.sem[which], n)
        self.cnt[which] += n

    def wait(self, eng, which, val=None):
        eng.wait_ge(self.sem[which], self.cnt[which] if val is None else val)


SAFE_ACT = True
SAFE_DVE = os.environ.get("KSAFE_DVE", "1") == "1"

_EMIT_OPS = {"activation", "tensor_scalar", "tensor_tensor", "tensor_copy",
             "reciprocal",
             "scalar_tensor_tensor", "tensor_tensor_scan", "memset",
             "tensor_scalar_add", "sign", "square", "copy", "mul", "add"}


class SafeEng:

    def __init__(self, eng, b, name, strict):
        self._e, self._b, self._n, self._strict = eng, b, name, strict

    def __getattr__(self, k):
        f = getattr(self._e, k)
        if k in _EMIT_OPS:
            def g(*a, **kw):
                if self._strict:
                    self._b.wait(self._e, self._n)
                inst = f(*a, **kw)
                self._b.inc(inst, self._n)
                return inst
            return g
        return f


def build_nc():
    nc = bass.Bass("TRN2", target_bir_lowering=False,
                   detect_race_conditions=False)
    DT = nc.dram_tensor

    xT = DT("xT", [DM, SEQ], F32, kind="ExternalInput")
    ipwT = DT("ipwT", [DM, 2 * DI], F32, kind="ExternalInput")
    convT = DT("convT", [4, DI, DI], F32, kind="ExternalInput")
    dtwT = DT("dtwT", [DI, DI], F32, kind="ExternalInput")
    bcwT = DT("bcwT", [DI, 32], F32, kind="ExternalInput")
    outwT = DT("outwT", [DI, DM], F32, kind="ExternalInput")
    w1T = DT("w1T", [DM, DH], F32, kind="ExternalInput")
    w2T = DT("w2T", [DH, DM], F32, kind="ExternalInput")
    b1r = DT("b1r", [128, NHG], F32, kind="ExternalInput")
    b2r = DT("b2r", [128, NDG], F32, kind="ExternalInput")
    convbr = DT("convbr", [128, NIG], F32, kind="ExternalInput")
    dtbr = DT("dtbr", [128, NIG], F32, kind="ExternalInput")
    Dr = DT("Dr", [128, NIG], F32, kind="ExternalInput")
    alogr = DT("alogr", [128, NIG * NS], F32, kind="ExternalInput")
    maskT = DT("maskT", [DM, SEQ], BF16, kind="ExternalInput")
    outT = DT("outT", [DM, SEQ], F32, kind="ExternalOutput")
    dbg = os.environ.get("KDBG_X2") == "1"
    if dbg:
        dbgT = DT("dbgT", [128, NDG * SEQ], BF16, kind="ExternalOutput")

    s_ipw = DT("s_ipw", [NDG, 128, 2 * DI], BF16, kind="Internal")
    s_conv = DT("s_conv", [NIG, 4, 128, DI], BF16, kind="Internal")
    s_dtw = DT("s_dtw", [NIG, 128, DI], BF16, kind="Internal")
    s_bcw = DT("s_bcw", [NIG, 128, 32], BF16, kind="Internal")
    s_outw = DT("s_outw", [NIG, 128, DM], BF16, kind="Internal")
    s_qw1 = DT("s_qw1", [NDG, 128, DH], BF16, kind="Internal")
    s_qw2 = DT("s_qw2", [NHG, 128, DM], BF16, kind="Internal")
    s_x2 = DT("s_x2", [128, NDG * SEQ], BF16, kind="Internal")

    import contextlib
    stack = contextlib.ExitStack()
    with stack:
        x2c16 = stack.enter_context(nc.sbuf_tensor("x2c16", [128, NDG * TC], BF16))
        carry = stack.enter_context(nc.sbuf_tensor("carry", [128, NIG * NS], F32))
        aneg = stack.enter_context(nc.sbuf_tensor("aneg", [128, NIG * NS], F32))
        dvec = stack.enter_context(nc.sbuf_tensor("dvec", [128, NIG], F32))
        convb = stack.enter_context(nc.sbuf_tensor("convb", [128, NIG], F32))
        dtb = stack.enter_context(nc.sbuf_tensor("dtb", [128, NIG], F32))
        qb1 = stack.enter_context(nc.sbuf_tensor("qb1", [128, NHG], F32))
        qb2 = stack.enter_context(nc.sbuf_tensor("qb2", [128, NDG], F32))
        cpool = stack.enter_context(nc.sbuf_tensor("cpool", [128, 16], F32))
        ones_c16 = stack.enter_context(nc.sbuf_tensor("ones_c16", [128, 1], BF16))
        ones_r16 = stack.enter_context(nc.sbuf_tensor("ones_r16", [1, 128], BF16))
        xch16 = stack.enter_context(nc.sbuf_tensor("xch16", [128, NDG * TC], BF16))
        ln1_16 = stack.enter_context(nc.sbuf_tensor("ln1_16", [128, NDG * TC], BF16))
        big16 = stack.enter_context(nc.sbuf_tensor("big16", [128, NIG * 515], BF16))
        zs16 = stack.enter_context(nc.sbuf_tensor("zs16", [128, NIG * TC], BF16))
        dt16 = stack.enter_context(nc.sbuf_tensor("dt16", [128, NIG * TC], BF16))
        u16 = stack.enter_context(nc.sbuf_tensor("u16", [128, NIG * TC], BF16))
        y32 = stack.enter_context(nc.sbuf_tensor("y32", [128, NIG * TC], F32))
        wtb = stack.enter_context(nc.sbuf_tensor("wtb", [128, 8192], BF16))
        wt16 = wtb
        scr32 = stack.enter_context(nc.sbuf_tensor("scr32", [128, 2048], F32))
        wst32 = scr32
        bbc16 = stack.enter_context(nc.sbuf_tensor("bbc16", [128, TC], BF16))
        cbc16 = stack.enter_context(nc.sbuf_tensor("cbc16", [128, TC], BF16))
        bm16 = stack.enter_context(nc.sbuf_tensor("bm16", [32, TC], BF16))
        brow16 = stack.enter_context(nc.sbuf_tensor("brow16", [1, TC], BF16))
        crow16 = stack.enter_context(nc.sbuf_tensor("crow16", [1, TC], BF16))
        brow2 = stack.enter_context(nc.sbuf_tensor("brow2", [1, TC], BF16))
        crow2 = stack.enter_context(nc.sbuf_tensor("crow2", [1, TC], BF16))
        bbc2 = stack.enter_context(nc.sbuf_tensor("bbc2", [128, TC], BF16))
        cbc2 = stack.enter_context(nc.sbuf_tensor("cbc2", [128, TC], BF16))
        sq16 = stack.enter_context(nc.sbuf_tensor("sq16", [128, TC], BF16))
        qa_t = stack.enter_context(nc.sbuf_tensor("qa", [128, TC], F32))
        qs_t = stack.enter_context(nc.sbuf_tensor("qs", [128, TC], F32))
        ql_t = stack.enter_context(nc.sbuf_tensor("ql", [128, TC], F32))
        qf_t = stack.enter_context(nc.sbuf_tensor("qf", [128, TC], F32))
        qt_t = stack.enter_context(nc.sbuf_tensor("qt", [128, TC], F32))
        qm_t = stack.enter_context(nc.sbuf_tensor("qm", [128, TC], F32))
        qsc_t = stack.enter_context(nc.sbuf_tensor("qsc", [128, TC], F32))
        qisc_t = stack.enter_context(nc.sbuf_tensor("qisc", [128, TC], F32))
        qa2_t = stack.enter_context(nc.sbuf_tensor("qa2", [128, TC], F32))
        qs2_t = stack.enter_context(nc.sbuf_tensor("qs2", [128, TC], F32))
        ql2_t = stack.enter_context(nc.sbuf_tensor("ql2", [128, TC], F32))
        qf2_t = stack.enter_context(nc.sbuf_tensor("qf2", [128, TC], F32))
        qt2_t = stack.enter_context(nc.sbuf_tensor("qt2", [128, TC], F32))
        qm2_t = stack.enter_context(nc.sbuf_tensor("qm2", [128, TC], F32))
        qsc2_t = stack.enter_context(nc.sbuf_tensor("qsc2", [128, TC], F32))
        qisc2_t = stack.enter_context(nc.sbuf_tensor("qisc2", [128, TC], F32))
        sdA = scr32[:, 0:TC]
        sdB = scr32[:, TC:2 * TC]
        sst = scr32[:, 2 * TC:3 * TC]
        stmp = scr32[:, 3 * TC:4 * TC]
        mrow = stack.enter_context(nc.sbuf_tensor("mrow", [1, TC], F32))
        trow = stack.enter_context(nc.sbuf_tensor("trow", [1, TC], F32))
        rstd = stack.enter_context(nc.sbuf_tensor("rstd", [1, TC], F32))
        brow = stack.enter_context(nc.sbuf_tensor("brow", [1, TC], F32))
        rsth, rstl, broh, brol = brow16, crow16, brow2, crow2
        ps0 = stack.enter_context(nc.psum_tensor("ps0", [128, TC], F32))
        ps1 = stack.enter_context(nc.psum_tensor("ps1", [128, TC], F32))
        ps2 = stack.enter_context(nc.psum_tensor("ps2", [128, TC], F32))
        ps3 = stack.enter_context(nc.psum_tensor("ps3", [128, TC], F32))
        ps4 = stack.enter_context(nc.psum_tensor("ps4", [128, TC], F32))
        ps5 = stack.enter_context(nc.psum_tensor("ps5", [128, TC], F32))
        ps6 = stack.enter_context(nc.psum_tensor("ps6", [128, TC], F32))
        ps7 = stack.enter_context(nc.psum_tensor("ps7", [128, TC], F32))
        ps = [ps0, ps1, ps2, ps3, ps4, ps5, ps6, ps7]
        b = Ctr(nc)

        class WBlk:
            def __init__(self, blk):
                self._blk = blk

            def vector(self, fn):
                self._blk.vector(lambda v: fn(SafeEng(v, b, "dve", SAFE_DVE)))

            def scalar(self, fn):
                self._blk.scalar(lambda s: fn(SafeEng(s, b, "act", SAFE_ACT)))

            def gpsimd(self, fn):
                self._blk.gpsimd(lambda g: fn(SafeEng(g, b, "pool", True)))

            def __getattr__(self, k):
                return getattr(self._blk, k)

        import contextlib as _ctx

        @_ctx.contextmanager
        def Block(**kw):
            with nc.Block(**kw) as blk:
                yield WBlk(blk)

        CONSTS = [LN2, -LN2, 9 * LN2, -9 * LN2, 17 * LN2, -17 * LN2,
                  24 * LN2, -24 * LN2, -8 * LN2, 1e-38, 1e-5, -0.5, 0.5,
                  1e-6, 1e-4, 1.0 / 1024.0]
        for j, cv in enumerate(CONSTS):
            nc.const_aps.register(F32, F(cv), cpool[:, j:j + 1])

        with Block(no_gpsimd_drain=True) as blk:
            def _init_v(v):
                for j, cv in enumerate(CONSTS):
                    v.memset(cpool[:, j:j + 1], F(cv))
                v.memset(ones_c16[:], 1.0)
                v.memset(ones_r16[:], 1.0)
                v.memset(carry[:], 0.0)
            blk.vector(_init_v)

            def _init_dma(sync):
                for dst, src in ((convb, convbr), (dtb, dtbr), (dvec, Dr),
                                 (aneg, alogr)):
                    b.inc(sync.dma_start(dst[:], src[:]), "ld", 16)
                b.wait(sync, "ld")
            blk.sync(_init_dma)

        with Block(no_gpsimd_drain=True) as blk:
            blk.scalar(lambda s: s.activation(aneg[:], aneg[:], AF.Exp))
            def _negv(v):
                b.wait(v, "act")
                v.tensor_scalar(aneg[:], aneg[:], -1.0, None, OP.mult)
            blk.vector(_negv)

        LSCR = [(qa_t, qs_t, ql_t, qf_t, qt_t, qm_t, qsc_t, qisc_t),
                (qa2_t, qs2_t, ql2_t, qf2_t, qt2_t, qm2_t, qsc2_t, qisc2_t)]
        lad_state = {"ctr": 0, 0: 0, 1: 0}
        def ladder_prefix(blk, src_ap, W, bias_col, min_e_f, top, par):
            qaP, qsP, qlP, qfP, qtP, qmP = LSCR[par][0:6]
            A_, S_, V_, F_, T_, M_ = (qaP[:, 0:W], qsP[:, 0:W], qlP[:, 0:W],
                                      qfP[:, 0:W], qtP[:, 0:W], qmP[:, 0:W])

            def v1(v):
                b.wait(v, "dve", lad_state[par])
                b.wait(v, "pe", b.cnt["pe"])
                if bias_col is None:
                    tt = src_ap
                else:
                    v.tensor_scalar(T_, src_ap, bias_col, None, OP.add)
                    tt = T_
                v.tensor_scalar(S_, tt, 0.0, None, OP.is_ge)
                v.tensor_scalar(S_, S_, 2.0, -1.0, OP.mult, OP.add)
                v.tensor_tensor(A_, tt, S_, OP.mult)
                v.tensor_scalar(V_, A_, float(2.0 ** -min_e_f), None, OP.mult)
                step = top // 2
                first = True
                while step >= 1:
                    v.tensor_scalar(T_, V_, float(2.0 ** step), None, OP.is_ge)
                    v.tensor_scalar(M_, T_, float(2.0 ** -step - 1.0), 1.0,
                                    OP.mult, OP.add)
                    v.tensor_tensor(V_, V_, M_, OP.mult)
                    if first:
                        v.tensor_scalar(F_, T_, float(step), None, OP.mult)
                        first = False
                    else:
                        v.scalar_tensor_tensor(F_, T_, float(step), F_,
                                               OP.mult, OP.add)
                    step //= 2
            blk.vector(v1)

        def ladder_round(blk, W, levels, par, thr_ap=None, gate_relu=False,
                         out16=None, out32=None):
            qaP, qsP, qlP, qfP, qtP, qmP, qscP, qiscP = LSCR[par]
            A_, S_, F_, T_, M_ = (qaP[:, 0:W], qsP[:, 0:W], qfP[:, 0:W],
                                  qtP[:, 0:W], qmP[:, 0:W])
            G_ = T_
            SC, IS = qscP[:, 0:W], qiscP[:, 0:W]
            V_ = A_

            def v2(v):
                b.wait(v, "act")
                v.tensor_tensor(V_, A_, IS, OP.mult)
                v.tensor_scalar(V_, V_, 0.5, None, OP.add)
                step0 = float(levels[0])
                v.tensor_scalar(F_, V_, step0, None, OP.is_ge)
                v.tensor_scalar(F_, F_, step0, None, OP.mult)
                for step in levels[1:]:
                    v.tensor_scalar(T_, F_, float(step), None, OP.add)
                    v.tensor_tensor(T_, V_, T_, OP.is_ge)
                    v.scalar_tensor_tensor(F_, T_, float(step), F_,
                                           OP.mult, OP.add)
                v.tensor_tensor(M_, V_, F_, OP.subtract)
                next_ = 2 if levels[0] > 8 else 1
                for _ in range(next_):
                    v.tensor_scalar(T_, M_, 2.0, None, OP.is_ge)
                    v.scalar_tensor_tensor(F_, T_, 2.0, F_, OP.mult, OP.add)
                    v.scalar_tensor_tensor(M_, T_, -2.0, M_, OP.mult, OP.add)
                if thr_ap is None:
                    v.tensor_scalar(G_, M_, F(1.0 + 2.0 ** -30), None,
                                    OP.is_gt)
                else:
                    v.tensor_tensor(G_, M_, thr_ap, OP.is_gt)
                v.tensor_tensor(F_, F_, G_, OP.add)
                v.tensor_tensor(F_, F_, SC, OP.mult)
                if gate_relu:
                    v.tensor_scalar(S_, S_, 0.0, None, OP.max)
                last = None
                if out16 is not None:
                    last = v.tensor_tensor(out16, F_, S_, OP.mult)
                if out32 is not None:
                    last = v.tensor_tensor(out32, F_, S_, OP.mult)
                last
            blk.vector(v2)

        def emit_q43(blk, src_ap, W, bias_col=None, out16=None, out32=None):
            par = lad_state["ctr"] % 2
            lad_state["ctr"] += 1
            qfP, qscP, qiscP = LSCR[par][3], LSCR[par][6], LSCR[par][7]
            ladder_prefix(blk, src_ap, W, bias_col, -6.0, 16, par)

            qtP = LSCR[par][4]

            def v1b(v):
                v.tensor_scalar(qtP[:, 0:W], qfP[:, 0:W], -1.0, None, OP.mult)
            blk.vector(v1b)

            def s2(s):
                b.wait(s, "dve")
                s.activation(qiscP[:, 0:W], qfP[:, 0:W], AF.Exp,
                             bias=F(9 * LN2), scale=F(-LN2))
                s.activation(qscP[:, 0:W], qtP[:, 0:W], AF.Exp,
                             bias=F(-9 * LN2), scale=F(-LN2))
            blk.scalar(s2)
            ladder_round(blk, W, [8, 4, 2], par, out16=out16, out32=out32)
            lad_state[par] = b.cnt["dve"]

        def emit_qh(blk, src_ap, W, bias_col, out16):
            par = lad_state["ctr"] % 2
            lad_state["ctr"] += 1
            qlP, qfP, qmP, qscP, qiscP = (LSCR[par][2], LSCR[par][3],
                                          LSCR[par][5], LSCR[par][6],
                                          LSCR[par][7])
            ladder_prefix(blk, src_ap, W, bias_col, -14.0, 32, par)

            qtP = LSCR[par][4]

            def v15(v):
                v.tensor_scalar(qmP[:, 0:W], qfP[:, 0:W], 8.0, None, OP.max)
                v.tensor_tensor(qlP[:, 0:W], qmP[:, 0:W], qfP[:, 0:W],
                                OP.subtract)
                v.tensor_copy(qfP[:, 0:W], qmP[:, 0:W])
                v.tensor_scalar(qtP[:, 0:W], qmP[:, 0:W], -1.0, None, OP.mult)
            blk.vector(v15)

            def s2(s):
                b.wait(s, "dve")
                s.activation(qiscP[:, 0:W], qfP[:, 0:W], AF.Exp,
                             bias=F(17 * LN2), scale=F(-LN2))
                s.activation(qscP[:, 0:W], qtP[:, 0:W], AF.Exp,
                             bias=F(-17 * LN2), scale=F(-LN2))
                s.activation(qlP[:, 0:W], qlP[:, 0:W], AF.Exp,
                             bias=F(-8 * LN2), scale=F(-LN2))
            blk.scalar(s2)

            def v16(v):
                b.wait(v, "act")
                v.tensor_scalar(qlP[:, 0:W], qlP[:, 0:W], 1.0, None, OP.add)
            blk.vector(v16)
            ladder_round(blk, W, [8, 4, 2], par, thr_ap=qlP[:, 0:W],
                         gate_relu=True, out16=out16)
            lad_state[par] = b.cnt["dve"]

        def emit_q510(blk, src_ap, W, bias_col, out32):
            par = lad_state["ctr"] % 2
            lad_state["ctr"] += 1
            qfP, qscP, qiscP = LSCR[par][3], LSCR[par][6], LSCR[par][7]
            ladder_prefix(blk, src_ap, W, bias_col, -14.0, 32, par)

            qtP = LSCR[par][4]

            def v1b(v):
                v.tensor_scalar(qtP[:, 0:W], qfP[:, 0:W], -1.0, None, OP.mult)
            blk.vector(v1b)

            def s2(s):
                b.wait(s, "dve")
                s.activation(qiscP[:, 0:W], qfP[:, 0:W], AF.Exp,
                             bias=F(24 * LN2), scale=F(-LN2))
                s.activation(qscP[:, 0:W], qtP[:, 0:W], AF.Exp,
                             bias=F(-24 * LN2), scale=F(-LN2))
            blk.scalar(s2)
            ladder_round(blk, W, [1024, 512, 256, 128, 64, 32, 16, 8, 4, 2],
                         par, out32=out32)
            lad_state[par] = b.cnt["dve"]

        prep = []
        for g in range(NDG):
            prep.append(("cast", ipwT[g * 128:(g + 1) * 128, :], s_ipw[g],
                         2 * DI))
        for ig in range(NIG):
            for k in range(4):
                prep.append(("cast", convT[k, ig * 128:(ig + 1) * 128, :],
                             s_conv[ig, k], DI))
        for g in range(NIG):
            prep.append(("cast", dtwT[g * 128:(g + 1) * 128, :], s_dtw[g], DI))
            prep.append(("cast", bcwT[g * 128:(g + 1) * 128, :], s_bcw[g], 32))
            prep.append(("cast", outwT[g * 128:(g + 1) * 128, :], s_outw[g],
                         DM))
        for g in range(NDG):
            prep.append(("q43", w1T[g * 128:(g + 1) * 128, :], s_qw1[g], DH))
        for g in range(NHG):
            prep.append(("q43", w2T[g * 128:(g + 1) * 128, :], s_qw2[g], DM))

        chunks = []
        for kind, src_, dst, width in prep:
            for j in range(0, width, 2048):
                chunks.append((kind, src_, dst, width, j, min(2048, width - j)))
        ld_hist, dve_hist, st_hist = [], [], []
        with Block(no_gpsimd_drain=True) as blk:
            for idx, (kind, src_, dst, width, j, wdt) in enumerate(chunks):
                stage = y32[:, (idx % 4) * 2048:(idx % 4) * 2048 + wdt]
                outb = wtb[:, (idx % 2) * 2048:(idx % 2) * 2048 + wdt]

                def _ld(sync, src_=src_, j=j, wdt=wdt, stage=stage, idx=idx):
                    if idx >= 4:
                        b.wait(sync, "dve", dve_hist[idx - 4])
                    b.inc(sync.dma_start(stage, src_[:, j:j + wdt]), "ld", 16)
                blk.sync(_ld)
                ld_hist.append(b.cnt["ld"])

                if kind == "cast":
                    def _cv(v, stage=stage, outb=outb, idx=idx):
                        v.wait_ge(b.sem["ld"], ld_hist[idx])
                        if idx >= 2:
                            v.wait_ge(b.sem["st"], st_hist[idx - 2])
                        v.tensor_copy(outb, stage)
                    blk.vector(_cv)
                else:
                    def _w(s, idx=idx):
                        s.wait_ge(b.sem["ld"], ld_hist[idx])
                        if idx >= 2:
                            s.wait_ge(b.sem["st"], st_hist[idx - 2])
                    blk.scalar(_w)

                    def _wv(v, idx=idx):
                        v.wait_ge(b.sem["ld"], ld_hist[idx])
                        if idx >= 2:
                            v.wait_ge(b.sem["st"], st_hist[idx - 2])
                    blk.vector(_wv)
                    for q0 in range(0, wdt, TC):
                        qw = min(TC, wdt - q0)
                        emit_q43(blk, stage[:, q0:q0 + qw], qw,
                                 out16=outb[:, q0:q0 + qw])
                dve_hist.append(b.cnt["dve"])

                def _st(g_, dst=dst, j=j, wdt=wdt, outb=outb, idx=idx):
                    b.wait(g_, "dve", dve_hist[idx])
                    b.inc(g_.dma_start(dst[:, j:j + wdt], outb), "st", 16)
                blk.gpsimd(_st)
                st_hist.append(b.cnt["st"])

        with Block(no_gpsimd_drain=True) as blk:
            def _ldb(sync):
                b.wait(sync, "st")
                b.inc(sync.dma_start(wst32[:, 0:NHG], b1r[:]), "ld", 16)
                b.inc(sync.dma_start(wst32[:, NHG:NHG + NDG], b2r[:]),
                      "ld", 16)
            blk.sync(_ldb)
            blk.scalar(lambda s: b.wait(s, "ld"))
            blk.vector(lambda v: b.wait(v, "ld"))
            emit_q43(blk, wst32[:, 0:NHG + NDG], NHG + NDG,
                     out32=y32[:, 0:NHG + NDG])
            def _cpb(v):
                v.tensor_copy(qb1[:], y32[:, 0:NHG])
                v.tensor_copy(qb2[:], y32[:, NHG:NHG + NDG])
            blk.vector(_cpb)

        with Block(no_gpsimd_drain=True) as blk:
            blk.gpsimd(lambda g_: b.wait(g_, "st"))

        def ln_stats(slices16):
            n = 128 * len(slices16)
            with Block(no_gpsimd_drain=True) as blk:
                for i, sl in enumerate(slices16):
                    first, last = i == 0, i == len(slices16) - 1

                    def _sq(s, sl=sl, first=first):
                        if not first:
                            b.wait(s, "pe", b.cnt["pe"])
                        s.activation(sq16[:], sl, AF.Square)
                    blk.scalar(_sq)

                    def _mm(t, sl=sl, first=first, last=last):
                        t.matmul(ps6[0:1, :], ones_c16[:], sl,
                                 start=first, stop=last)
                        b.wait(t, "act")
                        b.inc(t.matmul(ps7[0:1, :], ones_c16[:], sq16[:],
                                       start=first, stop=last), "pe")
                    blk.tensor(_mm)

                def _stat_v(v):
                    b.wait(v, "pe")
                    v.tensor_scalar(mrow[:], ps6[0:1, :], F(1.0 / n), None,
                                    OP.mult)
                    v.tensor_scalar(trow[:], ps7[0:1, :], F(1.0 / n), None,
                                    OP.mult)
                    v.tensor_tensor(rstd[:], mrow[:], mrow[:], OP.mult)
                    v.tensor_tensor(trow[:], trow[:], rstd[:],
                                          OP.subtract)
                blk.vector(_stat_v)

                def _stat_s(s):
                    b.wait(s, "dve")
                    s.activation(trow[:], trow[:], AF.Ln, bias=F(1e-5))
                    b.wait(s, "act")
                    s.activation(rstd[:], trow[:], AF.Exp, scale=-0.5)
                blk.scalar(_stat_s)

                def _stat_v2(v):
                    b.wait(v, "act")
                    v.tensor_tensor(brow[:], mrow[:], rstd[:], OP.mult)
                    v.tensor_scalar(brow[:], brow[:], -1.0, None, OP.mult)
                    v.tensor_copy(rsth[:], rstd[:])
                    v.tensor_tensor(trow[:], rstd[:], rsth[:], OP.subtract)
                    v.tensor_copy(rstl[:], trow[:])
                    v.tensor_copy(broh[:], brow[:])
                    v.tensor_tensor(trow[:], brow[:], broh[:], OP.subtract)
                    v.tensor_copy(brol[:], trow[:])
                blk.vector(_stat_v2)

                def _bc(t):
                    b.wait(t, "dve")
                    t.matmul(ps6[:], ones_r16[:], rsth[:], start=True,
                             stop=False)
                    t.matmul(ps6[:], ones_r16[:], rstl[:], start=False,
                             stop=True)
                    t.matmul(ps7[:], ones_r16[:], broh[:], start=True,
                             stop=False)
                    b.inc(t.matmul(ps7[:], ones_r16[:], brol[:], start=False,
                                   stop=True), "pe")
                blk.tensor(_bc)

        def mm_phase(n_og, kgs, load_fn, rhs_fn, evac_fn, ogblk=8):
            nk = len(kgs)
            for ogb in range(0, n_og, ogblk):
                nb = min(ogblk, n_og - ogb)
                KB = max(1, 4096 // (nb * 128))
                half_w = KB * nb * 128
                pe_hist = []
                with Block(no_gpsimd_drain=True) as blk:
                    for bi in range(0, nk, KB):
                        kgrp = kgs[bi:bi + KB]
                        par = (bi // KB) % 2
                        base = par * half_w

                        def _ld(sync, kgrp=kgrp, base=base, bi=bi, nb=nb,
                                ogb=ogb, pe_hist=tuple(pe_hist)):
                            if len(pe_hist) >= 2:
                                b.wait(sync, "pe", pe_hist[-2])
                            for j, kg in enumerate(kgrp):
                                load_fn(sync, kg, ogb, nb,
                                        base + j * nb * 128)
                        blk.sync(_ld)

                        def _mm(t, kgrp=kgrp, base=base, bi=bi, nb=nb, nk=nk):
                            b.wait(t, "ld")
                            for j, kg in enumerate(kgrp):
                                for oi in range(nb):
                                    off = base + j * nb * 128 + oi * 128
                                    mm = t.matmul(
                                        ps[oi][:],
                                        wtb[:, off:off + 128],
                                        rhs_fn(kg),
                                        start=(bi + j == 0),
                                        stop=(bi + j == nk - 1))
                            b.inc(mm, "pe")
                        blk.tensor(_mm)
                        pe_hist.append(b.cnt["pe"])

                    def _ev(s, ogb=ogb, nb=nb):
                        b.wait(s, "pe")
                        for oi in range(nb):
                            evac_fn(s, ogb + oi, ps[oi])
                    blk.scalar(_ev)
                    yield blk, ogb, nb

        def run_mm(gen):
            for _ in gen:
                pass

        def mk_loader(scr, colw):
            def load(sync, kg, ogb, nb, off):
                b.inc(sync.dma_start(
                    wtb[:, off:off + nb * 128],
                    scr[kg][:, ogb * 128:ogb * 128 + nb * 128]), "ld", 16)
            return load

        for c in range(NCHUNK):
            t0 = c * TC

            with Block(no_gpsimd_drain=True) as blk:
                for dg in range(NDG):
                    def _lx(sync, dg=dg):
                        if dg > 0:
                            b.wait(sync, "dve", b.cnt["dve"])
                        b.inc(sync.dma_start(
                            sst, xT[dg * 128:(dg + 1) * 128, t0:t0 + TC]),
                            "ld", 16)
                    blk.sync(_lx)

                    def _cx(v, dg=dg):
                        b.wait(v, "ld")
                        v.tensor_copy(xch16[:, dg * TC:(dg + 1) * TC],
                                      sst)
                    blk.vector(_cx)

                def _pfx(g_, c=c):
                    for g in range(NIG):
                        if c == 0:
                            g_.memset(big16[:, g * 515:g * 515 + 3], 0.0)
                        else:
                            g_.tensor_copy(
                                big16[:, g * 515:g * 515 + 3],
                                big16[:, g * 515 + 512:g * 515 + 515])
                blk.gpsimd(_pfx)

            ln_stats([xch16[:, dg * TC:(dg + 1) * TC] for dg in range(NDG)])
            with Block(no_gpsimd_drain=True) as blk:
                def _ap1(v):
                    b.wait(v, "pe")
                    for dg in range(NDG):
                        sl = xch16[:, dg * TC:(dg + 1) * TC]
                        ot = ln1_16[:, dg * TC:(dg + 1) * TC]
                        v.tensor_tensor(stmp, sl, ps6[:], OP.mult)
                        v.tensor_tensor(ot, stmp, ps7[:], OP.add)
                blk.vector(_ap1)

            def ipw_evac(s, og, psum, c=c):
                if og < NIG:
                    b.wait(s, "pool", b.cnt["pool"])
                    dst = big16[:, og * 515 + 3:og * 515 + 515]
                    s.activation(dst, psum[:], AF.Copy)
                else:
                    g = og - NIG
                    s.activation(zs16[:, g * TC:(g + 1) * TC], psum[:],
                                 AF.Exp, scale=-1.0)
                    zsev.append((g, psum))

            zsev = []
            for _blk, _ogb, _nb in mm_phase(
                    2 * NIG, list(range(NDG)), mk_loader(s_ipw, None),
                    lambda kg: ln1_16[:, kg * TC:(kg + 1) * TC], ipw_evac):
                if zsev:
                    def _zfin(v, pairs=tuple(zsev)):
                        b.wait(v, "act")
                        for g, psum in pairs:
                            sl = zs16[:, g * TC:(g + 1) * TC]
                            v.tensor_scalar(stmp, sl, 1.0, None, OP.add)
                            v.reciprocal(stmp, stmp)
                            v.tensor_tensor(sl, stmp, psum[:], OP.mult)
                    _blk.vector(_zfin)
                    zsev = []

            def conv_load(sync, igk, ogb, nb, off):
                ig, k = igk
                b.inc(sync.dma_start(
                    wtb[:, off:off + nb * 128],
                    s_conv[ig, k][:, ogb * 128:ogb * 128 + nb * 128]),
                    "ld", 16)

            def conv_evac(s, og, psum):
                s.activation(u16[:, og * TC:(og + 1) * TC], psum[:],
                                   AF.Identity, bias=convb[:, og:og + 1])

            run_mm(mm_phase(NIG, [(ig, k) for ig in range(NIG)
                                  for k in range(4)],
                            conv_load,
                            lambda igk: big16[:, igk[0] * 515 + igk[1]:
                                              igk[0] * 515 + igk[1] + TC],
                            conv_evac))

            def dt_evac(s, og, psum):
                s.activation(dt16[:, og * TC:(og + 1) * TC], psum[:],
                             AF.Exp, scale=-1.0, bias=dtb[:, og:og + 1])

            for _blk, _ogb, _nb in mm_phase(
                    NIG, list(range(NIG)), mk_loader(s_dtw, None),
                    lambda kg: u16[:, kg * TC:(kg + 1) * TC], dt_evac):
                def _fin(v, ogs=tuple(range(_ogb, _ogb + _nb))):
                    b.wait(v, "act")
                    for og in ogs:
                        sl = dt16[:, og * TC:(og + 1) * TC]
                        v.tensor_scalar(stmp, sl, 1.0, None, OP.add)
                        v.reciprocal(sst, stmp)
                        v.tensor_copy(sl, sst)
                _blk.vector(_fin)

            with Block(no_gpsimd_drain=True) as blk:
                for kg in range(NIG):
                    def _lbc(sync, kg=kg):
                        if kg >= 1:
                            b.wait(sync, "pe", b.cnt["pe"])
                        b.inc(sync.dma_start(
                            wt16[:, (kg % 2) * 1024:(kg % 2) * 1024 + 32],
                            s_bcw[kg][:]), "ld", 16)
                    blk.sync(_lbc)

                    def _mbc(t, kg=kg):
                        b.wait(t, "ld")
                        b.inc(t.matmul(ps0[0:32, :],
                                       wt16[:, (kg % 2) * 1024:
                                            (kg % 2) * 1024 + 32],
                                       u16[:, kg * TC:(kg + 1) * TC],
                                       start=(kg == 0), stop=(kg == NIG - 1)),
                              "pe")
                    blk.tensor(_mbc)

                def _ebc(s):
                    b.wait(s, "pe")
                    s.activation(bm16[:], ps0[0:32, :], AF.Copy)
                blk.scalar(_ebc)

            with Block(no_gpsimd_drain=True) as blk:
                for g in range(NIG):
                    def _yi(s, g=g):
                        b.wait(s, "dve", b.cnt["dve"])
                        s.activation(y32[:, g * TC:(g + 1) * TC],
                                           u16[:, g * TC:(g + 1) * TC],
                                           AF.Identity,
                                           scale=dvec[:, g:g + 1])
                    blk.scalar(_yi)

                    def _du(v, g=g):
                        dsl = dt16[:, g * TC:(g + 1) * TC]
                        usl = u16[:, g * TC:(g + 1) * TC]
                        v.tensor_scalar(stmp, dsl, 1e-4, None, OP.max)
                        v.tensor_copy(dsl, stmp)
                        v.tensor_tensor(sdB, stmp, usl, OP.mult)
                        b.wait(v, "act")
                        v.tensor_copy(usl, sdB)
                    blk.vector(_du)
            du16 = u16

            evac_hist = {}
            gloop_hist = {}
            bcast_hist = {}
            with Block(no_gpsimd_drain=True) as blk:
                for si in range(NS):
                    par = si % 2
                    psB, psC = (ps4, ps5) if par == 0 else (ps6, ps7)
                    brw, crw = (brow16, crow16) if par == 0 else (brow2, crow2)
                    bbb, ccc = (bbc16, cbc16) if par == 0 else (bbc2, cbc2)

                    def _rows(sync, si=si, brw=brw, crw=crw):
                        if si >= 2:
                            b.wait(sync, "pe", bcast_hist[si - 2])
                        b.inc(sync.dma_start(brw[:], bm16[si:si + 1, :]),
                              "ld", 16)
                        b.inc(sync.dma_start(crw[:],
                                             bm16[16 + si:17 + si, :]),
                              "ld", 16)
                    blk.sync(_rows)

                    def _bc2(t, si=si, psB=psB, psC=psC, brw=brw, crw=crw):
                        b.wait(t, "ld")
                        if si >= 2:
                            b.wait(t, "act", evac_hist[si - 2])
                        t.matmul(psB[:], ones_r16[:], brw[:],
                                 start=True, stop=True)
                        b.inc(t.matmul(psC[:], ones_r16[:], crw[:],
                                       start=True, stop=True), "pe")
                    blk.tensor(_bc2)
                    bcast_hist[si] = b.cnt["pe"]

                    def _ebc2(s, si=si, psB=psB, psC=psC, bbb=bbb, ccc=ccc):
                        b.wait(s, "pe", bcast_hist[si])
                        if si >= 2:
                            b.wait(s, "dve", gloop_hist[si - 2])
                        s.activation(bbb[:], psB[:], AF.Copy)
                        s.activation(ccc[:], psC[:], AF.Copy)
                    blk.scalar(_ebc2)
                    evac_hist[si] = b.cnt["act"]

                    for g in range(NIG):
                        col = g * NS + si

                        tpar = qa_t if g % 2 == 0 else qa2_t

                        def _pA(g_, g=g, col=col, tpar=tpar):
                            b.wait(g_, "act", b.cnt["act"])
                            g_.tensor_scalar(tpar[:], 
                                             dt16[:, g * TC:(g + 1) * TC],
                                             aneg[:, col:col + 1], None,
                                             OP.mult)
                        blk.gpsimd(_pA)

                        def _dA(s, g=g, tpar=tpar):
                            b.wait(s, "dve", b.cnt["dve"])
                            b.wait(s, "pool")
                            s.activation(sdA, tpar[:], AF.Exp)
                        blk.scalar(_dA)

                        def _sc(v, g=g, col=col, si=si, bbb=bbb, ccc=ccc):
                            if g == 0:
                                v.wait_ge(b.sem["act"], evac_hist[si])
                            v.tensor_tensor(sdB,
                                            du16[:, g * TC:(g + 1) * TC],
                                            bbb[:], OP.mult)
                            v.tensor_scalar(sdB, sdB, F(1e-6), None,
                                            OP.add)
                            b.wait(v, "act")
                            v.tensor_tensor_scan(sst, sdA, sdB,
                                                 carry[:, col:col + 1],
                                                 OP.mult, OP.add)
                            v.tensor_copy(carry[:, col:col + 1],
                                          sst[:, TC - 1:TC])
                            v.tensor_tensor(stmp, sst, ccc[:],
                                            OP.mult)
                            v.tensor_tensor(
                                y32[:, g * TC:(g + 1) * TC],
                                y32[:, g * TC:(g + 1) * TC], stmp, OP.add)
                        blk.vector(_sc)
                    gloop_hist[si] = b.cnt["dve"]

            with Block(no_gpsimd_drain=True) as blk:
                def _gt(v):
                    for g in range(NIG):
                        v.tensor_tensor(zs16[:, g * TC:(g + 1) * TC],
                                              y32[:, g * TC:(g + 1) * TC],
                                              zs16[:, g * TC:(g + 1) * TC],
                                              OP.mult)
                blk.vector(_gt)

            def attn_evac(s, og, psum):
                s.activation(ln1_16[:, og * TC:(og + 1) * TC], psum[:],
                                   AF.Copy)

            run_mm(mm_phase(NDG, list(range(NIG)), mk_loader(s_outw, None),
                            lambda kg: zs16[:, kg * TC:(kg + 1) * TC],
                            attn_evac))
            attn16 = ln1_16

            ln_stats([attn16[:, dg * TC:(dg + 1) * TC] for dg in range(NDG)])
            with Block(no_gpsimd_drain=True) as blk:
                for dg in range(NDG):
                    def _x2(v, dg=dg):
                        if dg == 0:
                            b.wait(v, "pe")
                        else:
                            b.wait(v, "st", b.cnt["st"])
                        asl = attn16[:, dg * TC:(dg + 1) * TC]
                        xsl = xch16[:, dg * TC:(dg + 1) * TC]
                        v.tensor_tensor(stmp, asl, ps6[:], OP.mult)
                        v.tensor_tensor(stmp, stmp, ps7[:], OP.add)
                        v.tensor_tensor(sq16[:], stmp, xsl, OP.add)
                    blk.vector(_x2)

                    def _x2s(g_, dg=dg, t0=t0):
                        b.wait(g_, "dve")
                        b.inc(g_.dma_start(
                            s_x2[:, dg * SEQ + t0:dg * SEQ + t0 + TC],
                            sq16[:]), "st", 16)
                    blk.gpsimd(_x2s)

        if dbg:
            with Block(no_gpsimd_drain=True) as blk:
                def _dbg(g_):
                    b.inc(g_.dma_start(dbgT[:], x2_16[:]), "st", 16)
                    b.wait(g_, "st")
                blk.gpsimd(_dbg)

        def qh_sl(og):
            if og < NIG:
                return dt16[:, og * TC:(og + 1) * TC]
            return u16[:, (og - NIG) * TC:(og - NIG + 1) * TC]

        for c in range(NCHUNK):
            t0 = c * TC
            x2sl = [x2c16[:, dg * TC:(dg + 1) * TC] for dg in range(NDG)]

            with Block(no_gpsimd_drain=True) as blk:
                def _lx2(sync, t0=t0):
                    b.wait(sync, "dve", b.cnt["dve"])
                    b.wait(sync, "st")
                    for dg in range(NDG):
                        b.inc(sync.dma_start(
                            x2c16[:, dg * TC:(dg + 1) * TC],
                            s_x2[:, dg * SEQ + t0:dg * SEQ + t0 + TC]),
                            "ld", 16)
                    b.wait(sync, "ld")
                blk.sync(_lx2)

            ln_stats(x2sl)
            for dg in range(NDG):
                with Block(no_gpsimd_drain=True) as blk:
                    def _l3(v, dg=dg):
                        b.wait(v, "pe")
                        v.tensor_tensor(stmp, x2sl[dg], ps6[:], OP.mult)
                        v.tensor_tensor(stmp, stmp, ps7[:],
                                              OP.add)
                    blk.vector(_l3)
                    emit_q43(blk, stmp, TC,
                             out16=xch16[:, dg * TC:(dg + 1) * TC])

            for ogb in range(0, NHG, 8):
                with Block(no_gpsimd_drain=True) as blk:
                    def _l1(sync, ogb=ogb):
                        for ki in range(NDG):
                            b.inc(sync.dma_start(
                                wtb[:, ki * 1024:(ki + 1) * 1024],
                                s_qw1[ki][:, ogb * 128:(ogb + 8) * 128]),
                                "ld", 16)
                    blk.sync(_l1)

                    def _m1(t):
                        b.wait(t, "ld")
                        for ki in range(NDG):
                            for oi in range(8):
                                off = ki * 1024 + oi * 128
                                mm = t.matmul(
                                    ps[oi][:],
                                    wtb[:, off:off + 128],
                                    xch16[:, ki * TC:(ki + 1) * TC],
                                    start=(ki == 0), stop=(ki == NDG - 1))
                        b.inc(mm, "pe")
                    blk.tensor(_m1)
                    blk.scalar(lambda s: b.wait(s, "pe"))
                    for oi in range(8):
                        og = ogb + oi
                        emit_qh(blk, ps[oi][:], TC, qb1[:, og:og + 1],
                                qh_sl(og))

            with Block(no_gpsimd_drain=True) as blk:
                pe_hist2 = []
                for bi in range(0, NHG, 4):
                    par = (bi // 4) % 2
                    base = par * 4096

                    def _l2(sync, bi=bi, base=base, pe_hist2=tuple(pe_hist2)):
                        if len(pe_hist2) >= 2:
                            b.wait(sync, "pe", pe_hist2[-2])
                        for j in range(4):
                            b.inc(sync.dma_start(
                                wtb[:, base + j * 1024:base + (j + 1) * 1024],
                                s_qw2[bi + j][:]), "ld", 16)
                    blk.sync(_l2)

                    def _m2(t, bi=bi, base=base):
                        b.wait(t, "ld")
                        for j in range(4):
                            for oi in range(NDG):
                                off = base + j * 1024 + oi * 128
                                mm = t.matmul(
                                    ps[oi][:],
                                    wtb[:, off:off + 128],
                                    qh_sl(bi + j),
                                    start=(bi + j == 0),
                                    stop=(bi + j == NHG - 1))
                        b.inc(mm, "pe")
                    blk.tensor(_m2)
                    pe_hist2.append(b.cnt["pe"])
                blk.scalar(lambda s: b.wait(s, "pe"))
                for dg in range(NDG):
                    emit_q510(blk, ps[dg][:], TC, qb2[:, dg:dg + 1],
                              y32[:, dg * TC:(dg + 1) * TC])

            with Block(no_gpsimd_drain=True) as blk:
                for dg in range(NDG):
                    def _lm(sync, dg=dg, t0=t0):
                        if dg > 0:
                            b.wait(sync, "dve", b.cnt["dve"])
                        b.inc(sync.dma_start(
                            sq16[:],
                            maskT[dg * 128:(dg + 1) * 128, t0:t0 + TC]),
                            "ld", 16)
                    blk.sync(_lm)

                    def _vm(v, dg=dg):
                        b.wait(v, "ld")
                        ysl = y32[:, dg * TC:(dg + 1) * TC]
                        v.tensor_tensor(ysl, ysl, sq16[:], OP.mult)
                        v.tensor_copy(ln1_16[:, dg * TC:(dg + 1) * TC],
                                            ysl)
                    blk.vector(_vm)

            ln_stats([ln1_16[:, dg * TC:(dg + 1) * TC] for dg in range(NDG)])
            with Block(no_gpsimd_drain=True) as blk:
                for dg in range(NDG):
                    def _o(v, dg=dg):
                        if dg == 0:
                            b.wait(v, "pe")
                        else:
                            b.wait(v, "st", b.cnt["st"])
                        ysl = y32[:, dg * TC:(dg + 1) * TC]
                        v.tensor_tensor(stmp, ysl, ps6[:], OP.mult)
                        v.tensor_tensor(stmp, stmp, ps7[:], OP.add)
                        v.tensor_tensor(sst, stmp, x2sl[dg],
                                              OP.add)
                    blk.vector(_o)

                    def _so(g_, dg=dg, t0=t0):
                        b.wait(g_, "dve")
                        b.inc(g_.dma_start(
                            outT[dg * 128:(dg + 1) * 128, t0:t0 + TC],
                            sst), "st", 16)
                    blk.gpsimd(_so)

        with Block(no_gpsimd_drain=True) as blk:
            blk.gpsimd(lambda g_: b.wait(g_, "st"))

    return nc


def kernel(**inputs):
    fp64_mode = os.environ.get("BASS_SIM_FP64") == "1"
    keep = _keep_mask(fp64_mode)
    maskval = np.where(keep, np.float32(1.25), np.float32(0.0)).astype(NPBF)

    x = np.ascontiguousarray(inputs["L0_attn_x"])
    ipwT = np.ascontiguousarray(inputs["L0_attn_in_proj_w"].T)
    convT = np.ascontiguousarray(
        np.transpose(inputs["L0_attn_conv_w"], (2, 1, 0)))
    dtwT = np.ascontiguousarray(inputs["L0_attn_dt_w"].T)
    bcwT = np.ascontiguousarray(
        np.concatenate([inputs["L0_attn_B_w"].T, inputs["L0_attn_C_w"].T],
                       axis=1))
    outwT = np.ascontiguousarray(inputs["L0_attn_out_w"].T)
    w1T = np.ascontiguousarray(inputs["L0_ffn_w1"].T)
    w2T = np.ascontiguousarray(inputs["L0_ffn_w2"].T)
    b1r = np.ascontiguousarray(inputs["L0_ffn_b1"].reshape(NHG, 128).T)
    b2r = np.ascontiguousarray(inputs["L0_ffn_b2"].reshape(NDG, 128).T)
    convbr = np.ascontiguousarray(
        inputs["L0_attn_conv_b"].reshape(NIG, 128).T)
    dtbr = np.ascontiguousarray(inputs["L0_attn_dt_b"].reshape(NIG, 128).T)
    Dr = np.ascontiguousarray(inputs["L0_attn_D"].reshape(NIG, 128).T)
    alogr = np.ascontiguousarray(
        inputs["L0_attn_A_log"].reshape(NIG, 128, NS).transpose(1, 0, 2)
        .reshape(128, NIG * NS))

    nc = build_nc()
    in_maps = []
    for core in range(8):
        in_maps.append({
            "xT": np.ascontiguousarray(x[core].T),
            "ipwT": ipwT, "convT": convT, "dtwT": dtwT, "bcwT": bcwT,
            "outwT": outwT, "w1T": w1T, "w2T": w2T, "b1r": b1r, "b2r": b2r,
            "convbr": convbr, "dtbr": dtbr, "Dr": Dr, "alogr": alogr,
            "maskT": np.ascontiguousarray(maskval[core].T),
        })
    res = run_bass_kernel_spmd(nc, in_maps, core_ids=list(range(8)))
    out = np.empty((8, SEQ, DM), np.float32)
    for core in range(8):
        out[core] = res.results[core]["outT"].T
    return out


# version 24
# speedup vs baseline: 1.2577x; 1.2577x over previous
import os
import numpy as np
import ml_dtypes
import concourse.bass as bass
import concourse.mybir as mybir
from concourse.bass_utils import run_bass_kernel_spmd

F32 = mybir.dt.float32
BF16 = mybir.dt.bfloat16
AF = mybir.ActivationFunctionType
OP = mybir.AluOpType
NPBF = ml_dtypes.bfloat16

SEQ = 2048
DM = 1024
DI = 2048
DH = 4096
NS = 16
TC = 512
NCHUNK = SEQ // TC
NDG = DM // 128
NIG = DI // 128
NHG = DH // 128

LN2 = float(np.log(2.0))
F = lambda v: float(np.float32(v))


def _threefry_bits(n):
    idx = np.arange(n, dtype=np.uint64)
    x0 = (idx >> np.uint64(32)).astype(np.uint32)
    x1 = (idx & np.uint64(0xFFFFFFFF)).astype(np.uint32)
    rot = [np.uint32(r) for r in (13, 15, 26, 6, 17, 29, 16, 24)]
    ks0 = np.uint32(0)
    ks1 = np.uint32(42)
    ks2 = np.uint32(np.uint32(0x1BD11BDA) ^ ks0 ^ ks1)

    def rotl(x, r):
        return (x << r) | (x >> (np.uint32(32) - r))

    def rounds(x0, x1, rs):
        for r in rs:
            x0 = x0 + x1
            x1 = rotl(x1, r)
            x1 = x1 ^ x0
        return x0, x1

    with np.errstate(over="ignore"):
        x0 = x0 + ks0
        x1 = x1 + ks1
        x0, x1 = rounds(x0, x1, rot[:4])
        x0 = x0 + ks1
        x1 = x1 + ks2 + np.uint32(1)
        x0, x1 = rounds(x0, x1, rot[4:])
        x0 = x0 + ks2
        x1 = x1 + ks0 + np.uint32(2)
        x0, x1 = rounds(x0, x1, rot[:4])
        x0 = x0 + ks0
        x1 = x1 + ks1 + np.uint32(3)
        x0, x1 = rounds(x0, x1, rot[4:])
        x0 = x0 + ks1
        x1 = x1 + ks2 + np.uint32(4)
        x0, x1 = rounds(x0, x1, rot[:4])
        x0 = x0 + ks2
        x1 = x1 + ks0 + np.uint32(5)
    return x0, x1


def _keep_mask(fp64_mode):
    n = 8 * SEQ * DM
    b1, b2 = _threefry_bits(n)
    if fp64_mode:
        u64 = (b1.astype(np.uint64) << np.uint64(32)) | b2.astype(np.uint64)
        fb = (u64 >> np.uint64(12)) | np.uint64(0x3FF0000000000000)
        keep = (fb.view(np.float64) - 1.0) < 0.8
    else:
        bits = b1 ^ b2
        fb = (bits >> np.uint32(9)) | np.uint32(0x3F800000)
        keep = (fb.view(np.float32) - np.float32(1.0)) < np.float32(0.8)
    return keep.reshape(8, SEQ, DM)


class Ctr:

    def __init__(self, nc):
        self.sem = {}
        self.cnt = {}
        for name in ("pe", "act", "dve", "pool", "ld", "st"):
            self.sem[name] = nc.alloc_semaphore("sem_" + name)
            self.cnt[name] = 0

    def inc(self, inst, which, n=1):
        inst.then_inc(self.sem[which], n)
        self.cnt[which] += n

    def wait(self, eng, which, val=None):
        eng.wait_ge(self.sem[which], self.cnt[which] if val is None else val)


SAFE_ACT = True
SAFE_DVE = os.environ.get("KSAFE_DVE", "1") == "1"

_EMIT_OPS = {"activation", "tensor_scalar", "tensor_tensor", "tensor_copy",
             "reciprocal",
             "scalar_tensor_tensor", "tensor_tensor_scan", "memset",
             "tensor_scalar_add", "sign", "square", "copy", "mul", "add"}


class SafeEng:

    def __init__(self, eng, b, name, strict):
        self._e, self._b, self._n, self._strict = eng, b, name, strict

    def __getattr__(self, k):
        f = getattr(self._e, k)
        if k in _EMIT_OPS:
            def g(*a, **kw):
                if self._strict:
                    self._b.wait(self._e, self._n)
                inst = f(*a, **kw)
                self._b.inc(inst, self._n)
                return inst
            return g
        return f


def build_nc():
    nc = bass.Bass("TRN2", target_bir_lowering=False,
                   detect_race_conditions=False)
    DT = nc.dram_tensor

    xT = DT("xT", [DM, SEQ], F32, kind="ExternalInput")
    ipwT = DT("ipwT", [DM, 2 * DI], F32, kind="ExternalInput")
    convT = DT("convT", [4, DI, DI], F32, kind="ExternalInput")
    dtwT = DT("dtwT", [DI, DI], F32, kind="ExternalInput")
    bcwT = DT("bcwT", [DI, 32], F32, kind="ExternalInput")
    outwT = DT("outwT", [DI, DM], F32, kind="ExternalInput")
    w1T = DT("w1T", [DM, DH], F32, kind="ExternalInput")
    w2T = DT("w2T", [DH, DM], F32, kind="ExternalInput")
    b1r = DT("b1r", [128, NHG], F32, kind="ExternalInput")
    b2r = DT("b2r", [128, NDG], F32, kind="ExternalInput")
    convbr = DT("convbr", [128, NIG], F32, kind="ExternalInput")
    dtbr = DT("dtbr", [128, NIG], F32, kind="ExternalInput")
    Dr = DT("Dr", [128, NIG], F32, kind="ExternalInput")
    alogr = DT("alogr", [128, NIG * NS], F32, kind="ExternalInput")
    maskT = DT("maskT", [DM, SEQ], BF16, kind="ExternalInput")
    outT = DT("outT", [DM, SEQ], F32, kind="ExternalOutput")
    dbg = os.environ.get("KDBG_X2") == "1"
    if dbg:
        dbgT = DT("dbgT", [128, NDG * SEQ], BF16, kind="ExternalOutput")

    s_ipw = DT("s_ipw", [NDG, 128, 2 * DI], BF16, kind="Internal")
    s_conv = DT("s_conv", [NIG, 4, 128, DI], BF16, kind="Internal")
    s_dtw = DT("s_dtw", [NIG, 128, DI], BF16, kind="Internal")
    s_bcw = DT("s_bcw", [NIG, 128, 32], BF16, kind="Internal")
    s_outw = DT("s_outw", [NIG, 128, DM], BF16, kind="Internal")
    s_qw1 = DT("s_qw1", [NDG, 128, DH], BF16, kind="Internal")
    s_qw2 = DT("s_qw2", [NHG, 128, DM], BF16, kind="Internal")
    s_x2 = DT("s_x2", [128, NDG * SEQ], BF16, kind="Internal")

    import contextlib
    stack = contextlib.ExitStack()
    with stack:
        x2c16 = stack.enter_context(nc.sbuf_tensor("x2c16", [128, NDG * TC], BF16))
        carry = stack.enter_context(nc.sbuf_tensor("carry", [128, NIG * NS], F32))
        aneg = stack.enter_context(nc.sbuf_tensor("aneg", [128, NIG * NS], F32))
        dvec = stack.enter_context(nc.sbuf_tensor("dvec", [128, NIG], F32))
        convb = stack.enter_context(nc.sbuf_tensor("convb", [128, NIG], F32))
        dtb = stack.enter_context(nc.sbuf_tensor("dtb", [128, NIG], F32))
        qb1 = stack.enter_context(nc.sbuf_tensor("qb1", [128, NHG], F32))
        qb2 = stack.enter_context(nc.sbuf_tensor("qb2", [128, NDG], F32))
        cpool = stack.enter_context(nc.sbuf_tensor("cpool", [128, 16], F32))
        ones_c16 = stack.enter_context(nc.sbuf_tensor("ones_c16", [128, 1], BF16))
        ones_r16 = stack.enter_context(nc.sbuf_tensor("ones_r16", [1, 128], BF16))
        xch16 = stack.enter_context(nc.sbuf_tensor("xch16", [128, NDG * TC], BF16))
        ln1_16 = stack.enter_context(nc.sbuf_tensor("ln1_16", [128, NDG * TC], BF16))
        big16 = stack.enter_context(nc.sbuf_tensor("big16", [128, NIG * 515], BF16))
        zs16 = stack.enter_context(nc.sbuf_tensor("zs16", [128, NIG * TC], BF16))
        dt16 = stack.enter_context(nc.sbuf_tensor("dt16", [128, NIG * TC], BF16))
        u16 = stack.enter_context(nc.sbuf_tensor("u16", [128, NIG * TC], BF16))
        y32 = stack.enter_context(nc.sbuf_tensor("y32", [128, NIG * TC], F32))
        wtb = stack.enter_context(nc.sbuf_tensor("wtb", [128, 8192], BF16))
        wt16 = wtb
        scr32 = stack.enter_context(nc.sbuf_tensor("scr32", [128, 2048], F32))
        wst32 = scr32
        bbc16 = stack.enter_context(nc.sbuf_tensor("bbc16", [128, TC], BF16))
        cbc16 = stack.enter_context(nc.sbuf_tensor("cbc16", [128, TC], BF16))
        bm16 = stack.enter_context(nc.sbuf_tensor("bm16", [32, TC], BF16))
        brow16 = stack.enter_context(nc.sbuf_tensor("brow16", [1, TC], BF16))
        crow16 = stack.enter_context(nc.sbuf_tensor("crow16", [1, TC], BF16))
        brow2 = stack.enter_context(nc.sbuf_tensor("brow2", [1, TC], BF16))
        crow2 = stack.enter_context(nc.sbuf_tensor("crow2", [1, TC], BF16))
        bbc2 = stack.enter_context(nc.sbuf_tensor("bbc2", [128, TC], BF16))
        cbc2 = stack.enter_context(nc.sbuf_tensor("cbc2", [128, TC], BF16))
        sq16 = stack.enter_context(nc.sbuf_tensor("sq16", [128, TC], BF16))
        qa_t = stack.enter_context(nc.sbuf_tensor("qa", [128, TC], F32))
        qs_t = stack.enter_context(nc.sbuf_tensor("qs", [128, TC], F32))
        ql_t = stack.enter_context(nc.sbuf_tensor("ql", [128, TC], F32))
        qf_t = stack.enter_context(nc.sbuf_tensor("qf", [128, TC], F32))
        qt_t = stack.enter_context(nc.sbuf_tensor("qt", [128, TC], F32))
        qm_t = stack.enter_context(nc.sbuf_tensor("qm", [128, TC], F32))
        qsc_t = stack.enter_context(nc.sbuf_tensor("qsc", [128, TC], F32))
        qisc_t = stack.enter_context(nc.sbuf_tensor("qisc", [128, TC], F32))
        qa2_t = stack.enter_context(nc.sbuf_tensor("qa2", [128, TC], F32))
        qs2_t = stack.enter_context(nc.sbuf_tensor("qs2", [128, TC], F32))
        ql2_t = stack.enter_context(nc.sbuf_tensor("ql2", [128, TC], F32))
        qf2_t = stack.enter_context(nc.sbuf_tensor("qf2", [128, TC], F32))
        qt2_t = stack.enter_context(nc.sbuf_tensor("qt2", [128, TC], F32))
        qm2_t = stack.enter_context(nc.sbuf_tensor("qm2", [128, TC], F32))
        qsc2_t = stack.enter_context(nc.sbuf_tensor("qsc2", [128, TC], F32))
        qisc2_t = stack.enter_context(nc.sbuf_tensor("qisc2", [128, TC], F32))
        sdA = scr32[:, 0:TC]
        sdB = scr32[:, TC:2 * TC]
        sst = scr32[:, 2 * TC:3 * TC]
        stmp = scr32[:, 3 * TC:4 * TC]
        mrow = stack.enter_context(nc.sbuf_tensor("mrow", [1, TC], F32))
        trow = stack.enter_context(nc.sbuf_tensor("trow", [1, TC], F32))
        rstd = stack.enter_context(nc.sbuf_tensor("rstd", [1, TC], F32))
        brow = stack.enter_context(nc.sbuf_tensor("brow", [1, TC], F32))
        rsth, rstl, broh, brol = brow16, crow16, brow2, crow2
        ps0 = stack.enter_context(nc.psum_tensor("ps0", [128, TC], F32))
        ps1 = stack.enter_context(nc.psum_tensor("ps1", [128, TC], F32))
        ps2 = stack.enter_context(nc.psum_tensor("ps2", [128, TC], F32))
        ps3 = stack.enter_context(nc.psum_tensor("ps3", [128, TC], F32))
        ps4 = stack.enter_context(nc.psum_tensor("ps4", [128, TC], F32))
        ps5 = stack.enter_context(nc.psum_tensor("ps5", [128, TC], F32))
        ps6 = stack.enter_context(nc.psum_tensor("ps6", [128, TC], F32))
        ps7 = stack.enter_context(nc.psum_tensor("ps7", [128, TC], F32))
        ps = [ps0, ps1, ps2, ps3, ps4, ps5, ps6, ps7]
        b = Ctr(nc)

        class WBlk:
            def __init__(self, blk):
                self._blk = blk

            def vector(self, fn):
                self._blk.vector(lambda v: fn(SafeEng(v, b, "dve", SAFE_DVE)))

            def scalar(self, fn):
                self._blk.scalar(lambda s: fn(SafeEng(s, b, "act", SAFE_ACT)))

            def gpsimd(self, fn):
                self._blk.gpsimd(lambda g: fn(SafeEng(g, b, "pool", True)))

            def __getattr__(self, k):
                return getattr(self._blk, k)

        import contextlib as _ctx

        @_ctx.contextmanager
        def Block(**kw):
            with nc.Block(**kw) as blk:
                yield WBlk(blk)

        CONSTS = [LN2, -LN2, 9 * LN2, -9 * LN2, 17 * LN2, -17 * LN2,
                  24 * LN2, -24 * LN2, -8 * LN2, 1e-38, 1e-5, -0.5, 0.5,
                  1e-6, 1e-4, 1.0 / 1024.0]
        for j, cv in enumerate(CONSTS):
            nc.const_aps.register(F32, F(cv), cpool[:, j:j + 1])

        with Block(no_gpsimd_drain=True) as blk:
            def _init_v(v):
                for j, cv in enumerate(CONSTS):
                    v.memset(cpool[:, j:j + 1], F(cv))
                v.memset(ones_c16[:], 1.0)
                v.memset(ones_r16[:], 1.0)
                v.memset(carry[:], 0.0)
            blk.vector(_init_v)

            def _init_dma(sync):
                for dst, src in ((convb, convbr), (dtb, dtbr), (dvec, Dr),
                                 (aneg, alogr)):
                    b.inc(sync.dma_start(dst[:], src[:]), "ld", 16)
                b.wait(sync, "ld")
            blk.sync(_init_dma)

        with Block(no_gpsimd_drain=True) as blk:
            blk.scalar(lambda s: s.activation(aneg[:], aneg[:], AF.Exp))
            def _negv(v):
                b.wait(v, "act")
                v.tensor_scalar(aneg[:], aneg[:], -1.0, None, OP.mult)
            blk.vector(_negv)

        LSCR = [(qa_t, qs_t, ql_t, qf_t, qt_t, qm_t, qsc_t, qisc_t),
                (qa2_t, qs2_t, ql2_t, qf2_t, qt2_t, qm2_t, qsc2_t, qisc2_t)]
        lad_state = {"ctr": 0, 0: 0, 1: 0}
        def ladder_prefix(blk, src_ap, W, bias_col, min_e_f, top, par):
            qaP, qsP, qlP, qfP, qtP, qmP = LSCR[par][0:6]
            A_, S_, V_, F_, T_, M_ = (qaP[:, 0:W], qsP[:, 0:W], qlP[:, 0:W],
                                      qfP[:, 0:W], qtP[:, 0:W], qmP[:, 0:W])

            def v1(v):
                b.wait(v, "dve", lad_state[par])
                b.wait(v, "pe", b.cnt["pe"])
                if bias_col is None:
                    tt = src_ap
                else:
                    v.tensor_scalar(T_, src_ap, bias_col, None, OP.add)
                    tt = T_
                v.tensor_scalar(S_, tt, 0.0, None, OP.is_ge)
                v.tensor_scalar(S_, S_, 2.0, -1.0, OP.mult, OP.add)
                v.tensor_tensor(A_, tt, S_, OP.mult)
                v.tensor_scalar(V_, A_, float(2.0 ** -min_e_f), None, OP.mult)
                step = top // 2
                first = True
                while step >= 1:
                    v.tensor_scalar(T_, V_, float(2.0 ** step), None, OP.is_ge)
                    v.tensor_scalar(M_, T_, float(2.0 ** -step - 1.0), 1.0,
                                    OP.mult, OP.add)
                    v.tensor_tensor(V_, V_, M_, OP.mult)
                    if first:
                        v.tensor_scalar(F_, T_, float(step), None, OP.mult)
                        first = False
                    else:
                        v.scalar_tensor_tensor(F_, T_, float(step), F_,
                                               OP.mult, OP.add)
                    step //= 2
            blk.vector(v1)

        def ladder_round(blk, W, levels, par, thr_ap=None, gate_relu=False,
                         out16=None, out32=None):
            qaP, qsP, qlP, qfP, qtP, qmP, qscP, qiscP = LSCR[par]
            A_, S_, F_, T_, M_ = (qaP[:, 0:W], qsP[:, 0:W], qfP[:, 0:W],
                                  qtP[:, 0:W], qmP[:, 0:W])
            G_ = T_
            SC, IS = qscP[:, 0:W], qiscP[:, 0:W]
            V_ = A_

            def v2(v):
                b.wait(v, "act")
                v.tensor_tensor(V_, A_, IS, OP.mult)
                v.tensor_scalar(V_, V_, 0.5, None, OP.add)
                step0 = float(levels[0])
                v.tensor_scalar(F_, V_, step0, None, OP.is_ge)
                v.tensor_scalar(F_, F_, step0, None, OP.mult)
                for step in levels[1:]:
                    v.tensor_scalar(T_, F_, float(step), None, OP.add)
                    v.tensor_tensor(T_, V_, T_, OP.is_ge)
                    v.scalar_tensor_tensor(F_, T_, float(step), F_,
                                           OP.mult, OP.add)
                v.tensor_tensor(M_, V_, F_, OP.subtract)
                next_ = 2 if levels[0] > 8 else 1
                for _ in range(next_):
                    v.tensor_scalar(T_, M_, 2.0, None, OP.is_ge)
                    v.scalar_tensor_tensor(F_, T_, 2.0, F_, OP.mult, OP.add)
                    v.scalar_tensor_tensor(M_, T_, -2.0, M_, OP.mult, OP.add)
                if thr_ap is None:
                    v.tensor_scalar(G_, M_, F(1.0 + 2.0 ** -30), None,
                                    OP.is_gt)
                else:
                    v.tensor_tensor(G_, M_, thr_ap, OP.is_gt)
                v.tensor_tensor(F_, F_, G_, OP.add)
                v.tensor_tensor(F_, F_, SC, OP.mult)
                if gate_relu:
                    v.tensor_scalar(S_, S_, 0.0, None, OP.max)
                last = None
                if out16 is not None:
                    last = v.tensor_tensor(out16, F_, S_, OP.mult)
                if out32 is not None:
                    last = v.tensor_tensor(out32, F_, S_, OP.mult)
                last
            blk.vector(v2)

        def emit_q43(blk, src_ap, W, bias_col=None, out16=None, out32=None):
            par = lad_state["ctr"] % 2
            lad_state["ctr"] += 1
            qfP, qscP, qiscP = LSCR[par][3], LSCR[par][6], LSCR[par][7]
            ladder_prefix(blk, src_ap, W, bias_col, -6.0, 16, par)

            qtP = LSCR[par][4]

            def v1b(v):
                v.tensor_scalar(qtP[:, 0:W], qfP[:, 0:W], -1.0, None, OP.mult)
            blk.vector(v1b)

            def s2(s):
                b.wait(s, "dve")
                s.activation(qiscP[:, 0:W], qfP[:, 0:W], AF.Exp,
                             bias=F(9 * LN2), scale=F(-LN2))
                s.activation(qscP[:, 0:W], qtP[:, 0:W], AF.Exp,
                             bias=F(-9 * LN2), scale=F(-LN2))
            blk.scalar(s2)
            ladder_round(blk, W, [8, 4, 2], par, out16=out16, out32=out32)
            lad_state[par] = b.cnt["dve"]

        def emit_qh(blk, src_ap, W, bias_col, out16):
            par = lad_state["ctr"] % 2
            lad_state["ctr"] += 1
            qlP, qfP, qmP, qscP, qiscP = (LSCR[par][2], LSCR[par][3],
                                          LSCR[par][5], LSCR[par][6],
                                          LSCR[par][7])
            ladder_prefix(blk, src_ap, W, bias_col, -14.0, 32, par)

            qtP = LSCR[par][4]

            def v15(v):
                v.tensor_scalar(qmP[:, 0:W], qfP[:, 0:W], 8.0, None, OP.max)
                v.tensor_tensor(qlP[:, 0:W], qmP[:, 0:W], qfP[:, 0:W],
                                OP.subtract)
                v.tensor_copy(qfP[:, 0:W], qmP[:, 0:W])
                v.tensor_scalar(qtP[:, 0:W], qmP[:, 0:W], -1.0, None, OP.mult)
            blk.vector(v15)

            def s2(s):
                b.wait(s, "dve")
                s.activation(qiscP[:, 0:W], qfP[:, 0:W], AF.Exp,
                             bias=F(17 * LN2), scale=F(-LN2))
                s.activation(qscP[:, 0:W], qtP[:, 0:W], AF.Exp,
                             bias=F(-17 * LN2), scale=F(-LN2))
                s.activation(qlP[:, 0:W], qlP[:, 0:W], AF.Exp,
                             bias=F(-8 * LN2), scale=F(-LN2))
            blk.scalar(s2)

            def v16(v):
                b.wait(v, "act")
                v.tensor_scalar(qlP[:, 0:W], qlP[:, 0:W], 1.0, None, OP.add)
            blk.vector(v16)
            ladder_round(blk, W, [8, 4, 2], par, thr_ap=qlP[:, 0:W],
                         gate_relu=True, out16=out16)
            lad_state[par] = b.cnt["dve"]

        def emit_q510(blk, src_ap, W, bias_col, out32):
            par = lad_state["ctr"] % 2
            lad_state["ctr"] += 1
            qfP, qscP, qiscP = LSCR[par][3], LSCR[par][6], LSCR[par][7]
            ladder_prefix(blk, src_ap, W, bias_col, -14.0, 32, par)

            qtP = LSCR[par][4]

            def v1b(v):
                v.tensor_scalar(qtP[:, 0:W], qfP[:, 0:W], -1.0, None, OP.mult)
            blk.vector(v1b)

            def s2(s):
                b.wait(s, "dve")
                s.activation(qiscP[:, 0:W], qfP[:, 0:W], AF.Exp,
                             bias=F(24 * LN2), scale=F(-LN2))
                s.activation(qscP[:, 0:W], qtP[:, 0:W], AF.Exp,
                             bias=F(-24 * LN2), scale=F(-LN2))
            blk.scalar(s2)
            ladder_round(blk, W, [1024, 512, 256, 128, 64, 32, 16, 8, 4, 2],
                         par, out32=out32)
            lad_state[par] = b.cnt["dve"]

        prep = []
        for g in range(NDG):
            prep.append(("cast", ipwT[g * 128:(g + 1) * 128, :], s_ipw[g],
                         2 * DI))
        for ig in range(NIG):
            for k in range(4):
                prep.append(("cast", convT[k, ig * 128:(ig + 1) * 128, :],
                             s_conv[ig, k], DI))
        for g in range(NIG):
            prep.append(("cast", dtwT[g * 128:(g + 1) * 128, :], s_dtw[g], DI))
            prep.append(("cast", bcwT[g * 128:(g + 1) * 128, :], s_bcw[g], 32))
            prep.append(("cast", outwT[g * 128:(g + 1) * 128, :], s_outw[g],
                         DM))
        for g in range(NDG):
            prep.append(("q43", w1T[g * 128:(g + 1) * 128, :], s_qw1[g], DH))
        for g in range(NHG):
            prep.append(("q43", w2T[g * 128:(g + 1) * 128, :], s_qw2[g], DM))

        chunks = []
        for kind, src_, dst, width in prep:
            for j in range(0, width, 2048):
                chunks.append((kind, src_, dst, width, j, min(2048, width - j)))
        ld_hist, dve_hist, st_hist = [], [], []
        with Block(no_gpsimd_drain=True) as blk:
            for idx, (kind, src_, dst, width, j, wdt) in enumerate(chunks):
                stage = y32[:, (idx % 4) * 2048:(idx % 4) * 2048 + wdt]
                outb = wtb[:, (idx % 2) * 2048:(idx % 2) * 2048 + wdt]

                def _ld(sync, src_=src_, j=j, wdt=wdt, stage=stage, idx=idx):
                    if idx >= 4:
                        b.wait(sync, "dve", dve_hist[idx - 4])
                    b.inc(sync.dma_start(stage, src_[:, j:j + wdt]), "ld", 16)
                blk.sync(_ld)
                ld_hist.append(b.cnt["ld"])

                if kind == "cast":
                    def _cv(v, stage=stage, outb=outb, idx=idx):
                        v.wait_ge(b.sem["ld"], ld_hist[idx])
                        if idx >= 2:
                            v.wait_ge(b.sem["st"], st_hist[idx - 2])
                        v.tensor_copy(outb, stage)
                    blk.vector(_cv)
                else:
                    def _w(s, idx=idx):
                        s.wait_ge(b.sem["ld"], ld_hist[idx])
                        if idx >= 2:
                            s.wait_ge(b.sem["st"], st_hist[idx - 2])
                    blk.scalar(_w)

                    def _wv(v, idx=idx):
                        v.wait_ge(b.sem["ld"], ld_hist[idx])
                        if idx >= 2:
                            v.wait_ge(b.sem["st"], st_hist[idx - 2])
                    blk.vector(_wv)
                    for q0 in range(0, wdt, TC):
                        qw = min(TC, wdt - q0)
                        emit_q43(blk, stage[:, q0:q0 + qw], qw,
                                 out16=outb[:, q0:q0 + qw])
                dve_hist.append(b.cnt["dve"])

                def _st(g_, dst=dst, j=j, wdt=wdt, outb=outb, idx=idx):
                    b.wait(g_, "dve", dve_hist[idx])
                    b.inc(g_.dma_start(dst[:, j:j + wdt], outb), "st", 16)
                blk.gpsimd(_st)
                st_hist.append(b.cnt["st"])

        with Block(no_gpsimd_drain=True) as blk:
            def _ldb(sync):
                b.wait(sync, "st")
                b.inc(sync.dma_start(wst32[:, 0:NHG], b1r[:]), "ld", 16)
                b.inc(sync.dma_start(wst32[:, NHG:NHG + NDG], b2r[:]),
                      "ld", 16)
            blk.sync(_ldb)
            blk.scalar(lambda s: b.wait(s, "ld"))
            blk.vector(lambda v: b.wait(v, "ld"))
            emit_q43(blk, wst32[:, 0:NHG + NDG], NHG + NDG,
                     out32=y32[:, 0:NHG + NDG])
            def _cpb(v):
                v.tensor_copy(qb1[:], y32[:, 0:NHG])
                v.tensor_copy(qb2[:], y32[:, NHG:NHG + NDG])
            blk.vector(_cpb)

        with Block(no_gpsimd_drain=True) as blk:
            blk.gpsimd(lambda g_: b.wait(g_, "st"))

        def ln_stats(slices16):
            n = 128 * len(slices16)
            with Block(no_gpsimd_drain=True) as blk:
                for i, sl in enumerate(slices16):
                    first, last = i == 0, i == len(slices16) - 1

                    def _sq(s, sl=sl, first=first):
                        if not first:
                            b.wait(s, "pe", b.cnt["pe"])
                        s.activation(sq16[:], sl, AF.Square)
                    blk.scalar(_sq)

                    def _mm(t, sl=sl, first=first, last=last):
                        t.matmul(ps6[0:1, :], ones_c16[:], sl,
                                 start=first, stop=last)
                        b.wait(t, "act")
                        b.inc(t.matmul(ps7[0:1, :], ones_c16[:], sq16[:],
                                       start=first, stop=last), "pe")
                    blk.tensor(_mm)

                def _stat_v(v):
                    b.wait(v, "pe")
                    v.tensor_scalar(mrow[:], ps6[0:1, :], F(1.0 / n), None,
                                    OP.mult)
                    v.tensor_scalar(trow[:], ps7[0:1, :], F(1.0 / n), None,
                                    OP.mult)
                    v.tensor_tensor(rstd[:], mrow[:], mrow[:], OP.mult)
                    v.tensor_tensor(trow[:], trow[:], rstd[:],
                                          OP.subtract)
                blk.vector(_stat_v)

                def _stat_s(s):
                    b.wait(s, "dve")
                    s.activation(trow[:], trow[:], AF.Ln, bias=F(1e-5))
                    b.wait(s, "act")
                    s.activation(rstd[:], trow[:], AF.Exp, scale=-0.5)
                blk.scalar(_stat_s)

                def _stat_v2(v):
                    b.wait(v, "act")
                    v.tensor_tensor(brow[:], mrow[:], rstd[:], OP.mult)
                    v.tensor_scalar(brow[:], brow[:], -1.0, None, OP.mult)
                    v.tensor_copy(rsth[:], rstd[:])
                    v.tensor_tensor(trow[:], rstd[:], rsth[:], OP.subtract)
                    v.tensor_copy(rstl[:], trow[:])
                    v.tensor_copy(broh[:], brow[:])
                    v.tensor_tensor(trow[:], brow[:], broh[:], OP.subtract)
                    v.tensor_copy(brol[:], trow[:])
                blk.vector(_stat_v2)

                def _bc(t):
                    b.wait(t, "dve")
                    t.matmul(ps6[:], ones_r16[:], rsth[:], start=True,
                             stop=False)
                    t.matmul(ps6[:], ones_r16[:], rstl[:], start=False,
                             stop=True)
                    t.matmul(ps7[:], ones_r16[:], broh[:], start=True,
                             stop=False)
                    b.inc(t.matmul(ps7[:], ones_r16[:], brol[:], start=False,
                                   stop=True), "pe")
                blk.tensor(_bc)

        def mm_phase(n_og, kgs, load_fn, rhs_fn, evac_fn, ogblk=8):
            nk = len(kgs)
            for ogb in range(0, n_og, ogblk):
                nb = min(ogblk, n_og - ogb)
                KB = max(1, 4096 // (nb * 128))
                half_w = KB * nb * 128
                pe_hist = []
                with Block(no_gpsimd_drain=True) as blk:
                    for bi in range(0, nk, KB):
                        kgrp = kgs[bi:bi + KB]
                        par = (bi // KB) % 2
                        base = par * half_w

                        def _ld(sync, kgrp=kgrp, base=base, bi=bi, nb=nb,
                                ogb=ogb, pe_hist=tuple(pe_hist)):
                            if len(pe_hist) >= 2:
                                b.wait(sync, "pe", pe_hist[-2])
                            for j, kg in enumerate(kgrp):
                                load_fn(sync, kg, ogb, nb,
                                        base + j * nb * 128)
                        blk.sync(_ld)

                        def _mm(t, kgrp=kgrp, base=base, bi=bi, nb=nb, nk=nk):
                            b.wait(t, "ld")
                            for j, kg in enumerate(kgrp):
                                for oi in range(nb):
                                    off = base + j * nb * 128 + oi * 128
                                    mm = t.matmul(
                                        ps[oi][:],
                                        wtb[:, off:off + 128],
                                        rhs_fn(kg),
                                        start=(bi + j == 0),
                                        stop=(bi + j == nk - 1))
                            b.inc(mm, "pe")
                        blk.tensor(_mm)
                        pe_hist.append(b.cnt["pe"])

                    def _ev(s, ogb=ogb, nb=nb):
                        b.wait(s, "pe")
                        for oi in range(nb):
                            evac_fn(s, ogb + oi, ps[oi])
                    blk.scalar(_ev)
                    yield blk, ogb, nb

        def run_mm(gen):
            for _ in gen:
                pass

        def mk_loader(scr, colw):
            def load(sync, kg, ogb, nb, off):
                b.inc(sync.dma_start(
                    wtb[:, off:off + nb * 128],
                    scr[kg][:, ogb * 128:ogb * 128 + nb * 128]), "ld", 16)
            return load

        for c in range(NCHUNK):
            t0 = c * TC

            with Block(no_gpsimd_drain=True) as blk:
                for dg in range(NDG):
                    def _lx(sync, dg=dg):
                        if dg > 0:
                            b.wait(sync, "dve", b.cnt["dve"])
                        b.inc(sync.dma_start(
                            sst, xT[dg * 128:(dg + 1) * 128, t0:t0 + TC]),
                            "ld", 16)
                    blk.sync(_lx)

                    def _cx(v, dg=dg):
                        b.wait(v, "ld")
                        v.tensor_copy(xch16[:, dg * TC:(dg + 1) * TC],
                                      sst)
                    blk.vector(_cx)

                def _pfx(g_, c=c):
                    for g in range(NIG):
                        if c == 0:
                            g_.memset(big16[:, g * 515:g * 515 + 3], 0.0)
                        else:
                            g_.tensor_copy(
                                big16[:, g * 515:g * 515 + 3],
                                big16[:, g * 515 + 512:g * 515 + 515])
                blk.gpsimd(_pfx)

            ln_stats([xch16[:, dg * TC:(dg + 1) * TC] for dg in range(NDG)])
            with Block(no_gpsimd_drain=True) as blk:
                def _ap1(v):
                    b.wait(v, "pe")
                    for dg in range(NDG):
                        sl = xch16[:, dg * TC:(dg + 1) * TC]
                        ot = ln1_16[:, dg * TC:(dg + 1) * TC]
                        v.tensor_tensor(stmp, sl, ps6[:], OP.mult)
                        v.tensor_tensor(ot, stmp, ps7[:], OP.add)
                blk.vector(_ap1)

            def ipw_evac(s, og, psum, c=c):
                if og < NIG:
                    b.wait(s, "pool", b.cnt["pool"])
                    dst = big16[:, og * 515 + 3:og * 515 + 515]
                    s.activation(dst, psum[:], AF.Copy)
                else:
                    g = og - NIG
                    s.activation(zs16[:, g * TC:(g + 1) * TC], psum[:],
                                 AF.Exp, scale=-1.0)
                    zsev.append((g, psum))

            zsev = []
            for _blk, _ogb, _nb in mm_phase(
                    2 * NIG, list(range(NDG)), mk_loader(s_ipw, None),
                    lambda kg: ln1_16[:, kg * TC:(kg + 1) * TC], ipw_evac):
                if zsev:
                    def _zfin(v, pairs=tuple(zsev)):
                        b.wait(v, "act")
                        for g, psum in pairs:
                            sl = zs16[:, g * TC:(g + 1) * TC]
                            v.tensor_scalar(stmp, sl, 1.0, None, OP.add)
                            v.reciprocal(stmp, stmp)
                            v.tensor_tensor(sl, stmp, psum[:], OP.mult)
                    _blk.vector(_zfin)
                    zsev = []

            def conv_load(sync, igk, ogb, nb, off):
                ig, k = igk
                b.inc(sync.dma_start(
                    wtb[:, off:off + nb * 128],
                    s_conv[ig, k][:, ogb * 128:ogb * 128 + nb * 128]),
                    "ld", 16)

            def conv_evac(s, og, psum):
                s.activation(u16[:, og * TC:(og + 1) * TC], psum[:],
                                   AF.Identity, bias=convb[:, og:og + 1])

            run_mm(mm_phase(NIG, [(ig, k) for ig in range(NIG)
                                  for k in range(4)],
                            conv_load,
                            lambda igk: big16[:, igk[0] * 515 + igk[1]:
                                              igk[0] * 515 + igk[1] + TC],
                            conv_evac))

            def dt_evac(s, og, psum):
                s.activation(dt16[:, og * TC:(og + 1) * TC], psum[:],
                             AF.Exp, scale=-1.0, bias=dtb[:, og:og + 1])

            for _blk, _ogb, _nb in mm_phase(
                    NIG, list(range(NIG)), mk_loader(s_dtw, None),
                    lambda kg: u16[:, kg * TC:(kg + 1) * TC], dt_evac):
                def _fin(v, ogs=tuple(range(_ogb, _ogb + _nb))):
                    b.wait(v, "act")
                    for og in ogs:
                        sl = dt16[:, og * TC:(og + 1) * TC]
                        v.tensor_scalar(stmp, sl, 1.0, None, OP.add)
                        v.reciprocal(sst, stmp)
                        v.tensor_copy(sl, sst)
                _blk.vector(_fin)

            with Block(no_gpsimd_drain=True) as blk:
                for kg in range(NIG):
                    def _lbc(sync, kg=kg):
                        if kg >= 1:
                            b.wait(sync, "pe", b.cnt["pe"])
                        b.inc(sync.dma_start(
                            wt16[:, (kg % 2) * 1024:(kg % 2) * 1024 + 32],
                            s_bcw[kg][:]), "ld", 16)
                    blk.sync(_lbc)

                    def _mbc(t, kg=kg):
                        b.wait(t, "ld")
                        b.inc(t.matmul(ps0[0:32, :],
                                       wt16[:, (kg % 2) * 1024:
                                            (kg % 2) * 1024 + 32],
                                       u16[:, kg * TC:(kg + 1) * TC],
                                       start=(kg == 0), stop=(kg == NIG - 1)),
                              "pe")
                    blk.tensor(_mbc)

                def _ebc(s):
                    b.wait(s, "pe")
                    s.activation(bm16[:], ps0[0:32, :], AF.Copy)
                blk.scalar(_ebc)

            with Block(no_gpsimd_drain=True) as blk:
                for g in range(NIG):
                    def _yi(s, g=g):
                        b.wait(s, "dve", b.cnt["dve"])
                        s.activation(y32[:, g * TC:(g + 1) * TC],
                                           u16[:, g * TC:(g + 1) * TC],
                                           AF.Identity,
                                           scale=dvec[:, g:g + 1])
                    blk.scalar(_yi)

                    def _du(v, g=g):
                        dsl = dt16[:, g * TC:(g + 1) * TC]
                        usl = u16[:, g * TC:(g + 1) * TC]
                        v.tensor_scalar(stmp, dsl, 1e-4, None, OP.max)
                        v.tensor_copy(dsl, stmp)
                        v.tensor_tensor(sdB, stmp, usl, OP.mult)
                        b.wait(v, "act")
                        v.tensor_copy(usl, sdB)
                    blk.vector(_du)
            du16 = u16

            evac_hist = {}
            gloop_hist = {}
            bcast_hist = {}
            with Block(no_gpsimd_drain=True) as blk:
                for si in range(NS):
                    par = si % 2
                    psB, psC = (ps4, ps5) if par == 0 else (ps6, ps7)
                    brw, crw = (brow16, crow16) if par == 0 else (brow2, crow2)
                    bbb, ccc = (bbc16, cbc16) if par == 0 else (bbc2, cbc2)

                    def _rows(sync, si=si, brw=brw, crw=crw):
                        if si >= 2:
                            b.wait(sync, "pe", bcast_hist[si - 2])
                        b.inc(sync.dma_start(brw[:], bm16[si:si + 1, :]),
                              "ld", 16)
                        b.inc(sync.dma_start(crw[:],
                                             bm16[16 + si:17 + si, :]),
                              "ld", 16)
                    blk.sync(_rows)

                    def _bc2(t, si=si, psB=psB, psC=psC, brw=brw, crw=crw):
                        b.wait(t, "ld")
                        if si >= 2:
                            b.wait(t, "act", evac_hist[si - 2])
                        t.matmul(psB[:], ones_r16[:], brw[:],
                                 start=True, stop=True)
                        b.inc(t.matmul(psC[:], ones_r16[:], crw[:],
                                       start=True, stop=True), "pe")
                    blk.tensor(_bc2)
                    bcast_hist[si] = b.cnt["pe"]

                    def _ebc2(s, si=si, psB=psB, psC=psC, bbb=bbb, ccc=ccc):
                        b.wait(s, "pe", bcast_hist[si])
                        if si >= 2:
                            b.wait(s, "dve", gloop_hist[si - 2])
                        s.activation(bbb[:], psB[:], AF.Copy)
                        s.activation(ccc[:], psC[:], AF.Copy)
                    blk.scalar(_ebc2)
                    evac_hist[si] = b.cnt["act"]

                    for g in range(NIG):
                        col = g * NS + si

                        def _dA(s, g=g, col=col):
                            b.wait(s, "dve", b.cnt["dve"])
                            s.activation(
                                sdA, dt16[:, g * TC:(g + 1) * TC], AF.Exp,
                                scale=aneg[:, col:col + 1])
                        blk.scalar(_dA)

                        def _sc(v, g=g, col=col, si=si, bbb=bbb, ccc=ccc):
                            if g == 0:
                                v.wait_ge(b.sem["act"], evac_hist[si])
                            v.tensor_tensor(sdB,
                                            du16[:, g * TC:(g + 1) * TC],
                                            bbb[:], OP.mult)
                            v.tensor_scalar(sdB, sdB, F(1e-6), None,
                                            OP.add)
                            b.wait(v, "act")
                            v.tensor_tensor_scan(sst, sdA, sdB,
                                                 carry[:, col:col + 1],
                                                 OP.mult, OP.add)
                            v.tensor_copy(carry[:, col:col + 1],
                                          sst[:, TC - 1:TC])
                            v.tensor_tensor(stmp, sst, ccc[:],
                                            OP.mult)
                            v.tensor_tensor(
                                y32[:, g * TC:(g + 1) * TC],
                                y32[:, g * TC:(g + 1) * TC], stmp, OP.add)
                        blk.vector(_sc)
                    gloop_hist[si] = b.cnt["dve"]

            with Block(no_gpsimd_drain=True) as blk:
                def _gt(v):
                    for g in range(NIG):
                        v.tensor_tensor(zs16[:, g * TC:(g + 1) * TC],
                                              y32[:, g * TC:(g + 1) * TC],
                                              zs16[:, g * TC:(g + 1) * TC],
                                              OP.mult)
                blk.vector(_gt)

            def attn_evac(s, og, psum):
                s.activation(ln1_16[:, og * TC:(og + 1) * TC], psum[:],
                                   AF.Copy)

            run_mm(mm_phase(NDG, list(range(NIG)), mk_loader(s_outw, None),
                            lambda kg: zs16[:, kg * TC:(kg + 1) * TC],
                            attn_evac))
            attn16 = ln1_16

            ln_stats([attn16[:, dg * TC:(dg + 1) * TC] for dg in range(NDG)])
            with Block(no_gpsimd_drain=True) as blk:
                for dg in range(NDG):
                    def _x2(v, dg=dg):
                        if dg == 0:
                            b.wait(v, "pe")
                        else:
                            b.wait(v, "st", b.cnt["st"])
                        asl = attn16[:, dg * TC:(dg + 1) * TC]
                        xsl = xch16[:, dg * TC:(dg + 1) * TC]
                        v.tensor_tensor(stmp, asl, ps6[:], OP.mult)
                        v.tensor_tensor(stmp, stmp, ps7[:], OP.add)
                        v.tensor_tensor(sq16[:], stmp, xsl, OP.add)
                    blk.vector(_x2)

                    def _x2s(g_, dg=dg, t0=t0):
                        b.wait(g_, "dve")
                        b.inc(g_.dma_start(
                            s_x2[:, dg * SEQ + t0:dg * SEQ + t0 + TC],
                            sq16[:]), "st", 16)
                    blk.gpsimd(_x2s)

        if dbg:
            with Block(no_gpsimd_drain=True) as blk:
                def _dbg(g_):
                    b.inc(g_.dma_start(dbgT[:], x2_16[:]), "st", 16)
                    b.wait(g_, "st")
                blk.gpsimd(_dbg)

        def qh_sl(og):
            if og < NIG:
                return dt16[:, og * TC:(og + 1) * TC]
            return u16[:, (og - NIG) * TC:(og - NIG + 1) * TC]

        for c in range(NCHUNK):
            t0 = c * TC
            x2sl = [x2c16[:, dg * TC:(dg + 1) * TC] for dg in range(NDG)]

            with Block(no_gpsimd_drain=True) as blk:
                def _lx2(sync, t0=t0):
                    b.wait(sync, "dve", b.cnt["dve"])
                    b.wait(sync, "st")
                    for dg in range(NDG):
                        b.inc(sync.dma_start(
                            x2c16[:, dg * TC:(dg + 1) * TC],
                            s_x2[:, dg * SEQ + t0:dg * SEQ + t0 + TC]),
                            "ld", 16)
                    b.wait(sync, "ld")
                blk.sync(_lx2)

            ln_stats(x2sl)
            for dg in range(NDG):
                with Block(no_gpsimd_drain=True) as blk:
                    def _l3(v, dg=dg):
                        b.wait(v, "pe")
                        v.tensor_tensor(stmp, x2sl[dg], ps6[:], OP.mult)
                        v.tensor_tensor(stmp, stmp, ps7[:],
                                              OP.add)
                    blk.vector(_l3)
                    emit_q43(blk, stmp, TC,
                             out16=xch16[:, dg * TC:(dg + 1) * TC])

            for ogb in range(0, NHG, 8):
                with Block(no_gpsimd_drain=True) as blk:
                    def _l1(sync, ogb=ogb):
                        for ki in range(NDG):
                            b.inc(sync.dma_start(
                                wtb[:, ki * 1024:(ki + 1) * 1024],
                                s_qw1[ki][:, ogb * 128:(ogb + 8) * 128]),
                                "ld", 16)
                    blk.sync(_l1)

                    def _m1(t):
                        b.wait(t, "ld")
                        for ki in range(NDG):
                            for oi in range(8):
                                off = ki * 1024 + oi * 128
                                mm = t.matmul(
                                    ps[oi][:],
                                    wtb[:, off:off + 128],
                                    xch16[:, ki * TC:(ki + 1) * TC],
                                    start=(ki == 0), stop=(ki == NDG - 1))
                        b.inc(mm, "pe")
                    blk.tensor(_m1)
                    blk.scalar(lambda s: b.wait(s, "pe"))
                    for oi in range(8):
                        og = ogb + oi
                        emit_qh(blk, ps[oi][:], TC, qb1[:, og:og + 1],
                                qh_sl(og))

            with Block(no_gpsimd_drain=True) as blk:
                pe_hist2 = []
                for bi in range(0, NHG, 4):
                    par = (bi // 4) % 2
                    base = par * 4096

                    def _l2(sync, bi=bi, base=base, pe_hist2=tuple(pe_hist2)):
                        if len(pe_hist2) >= 2:
                            b.wait(sync, "pe", pe_hist2[-2])
                        for j in range(4):
                            b.inc(sync.dma_start(
                                wtb[:, base + j * 1024:base + (j + 1) * 1024],
                                s_qw2[bi + j][:]), "ld", 16)
                    blk.sync(_l2)

                    def _m2(t, bi=bi, base=base):
                        b.wait(t, "ld")
                        for j in range(4):
                            for oi in range(NDG):
                                off = base + j * 1024 + oi * 128
                                mm = t.matmul(
                                    ps[oi][:],
                                    wtb[:, off:off + 128],
                                    qh_sl(bi + j),
                                    start=(bi + j == 0),
                                    stop=(bi + j == NHG - 1))
                        b.inc(mm, "pe")
                    blk.tensor(_m2)
                    pe_hist2.append(b.cnt["pe"])
                blk.scalar(lambda s: b.wait(s, "pe"))
                for dg in range(NDG):
                    emit_q510(blk, ps[dg][:], TC, qb2[:, dg:dg + 1],
                              y32[:, dg * TC:(dg + 1) * TC])

            with Block(no_gpsimd_drain=True) as blk:
                for dg in range(NDG):
                    def _lm(sync, dg=dg, t0=t0):
                        if dg > 0:
                            b.wait(sync, "dve", b.cnt["dve"])
                        b.inc(sync.dma_start(
                            sq16[:],
                            maskT[dg * 128:(dg + 1) * 128, t0:t0 + TC]),
                            "ld", 16)
                    blk.sync(_lm)

                    def _vm(v, dg=dg):
                        b.wait(v, "ld")
                        ysl = y32[:, dg * TC:(dg + 1) * TC]
                        v.tensor_tensor(ysl, ysl, sq16[:], OP.mult)
                        v.tensor_copy(ln1_16[:, dg * TC:(dg + 1) * TC],
                                            ysl)
                    blk.vector(_vm)

            ln_stats([ln1_16[:, dg * TC:(dg + 1) * TC] for dg in range(NDG)])
            with Block(no_gpsimd_drain=True) as blk:
                for dg in range(NDG):
                    def _o(v, dg=dg):
                        if dg == 0:
                            b.wait(v, "pe")
                        else:
                            b.wait(v, "st", b.cnt["st"])
                        ysl = y32[:, dg * TC:(dg + 1) * TC]
                        v.tensor_tensor(stmp, ysl, ps6[:], OP.mult)
                        v.tensor_tensor(stmp, stmp, ps7[:], OP.add)
                        v.tensor_tensor(sst, stmp, x2sl[dg],
                                              OP.add)
                    blk.vector(_o)

                    def _so(g_, dg=dg, t0=t0):
                        b.wait(g_, "dve")
                        b.inc(g_.dma_start(
                            outT[dg * 128:(dg + 1) * 128, t0:t0 + TC],
                            sst), "st", 16)
                    blk.gpsimd(_so)

        with Block(no_gpsimd_drain=True) as blk:
            blk.gpsimd(lambda g_: b.wait(g_, "st"))

    return nc


def kernel(**inputs):
    fp64_mode = os.environ.get("BASS_SIM_FP64") == "1"
    keep = _keep_mask(fp64_mode)
    maskval = np.where(keep, np.float32(1.25), np.float32(0.0)).astype(NPBF)

    x = np.ascontiguousarray(inputs["L0_attn_x"])
    ipwT = np.ascontiguousarray(inputs["L0_attn_in_proj_w"].T)
    convT = np.ascontiguousarray(
        np.transpose(inputs["L0_attn_conv_w"], (2, 1, 0)))
    dtwT = np.ascontiguousarray(inputs["L0_attn_dt_w"].T)
    bcwT = np.ascontiguousarray(
        np.concatenate([inputs["L0_attn_B_w"].T, inputs["L0_attn_C_w"].T],
                       axis=1))
    outwT = np.ascontiguousarray(inputs["L0_attn_out_w"].T)
    w1T = np.ascontiguousarray(inputs["L0_ffn_w1"].T)
    w2T = np.ascontiguousarray(inputs["L0_ffn_w2"].T)
    b1r = np.ascontiguousarray(inputs["L0_ffn_b1"].reshape(NHG, 128).T)
    b2r = np.ascontiguousarray(inputs["L0_ffn_b2"].reshape(NDG, 128).T)
    convbr = np.ascontiguousarray(
        inputs["L0_attn_conv_b"].reshape(NIG, 128).T)
    dtbr = np.ascontiguousarray(inputs["L0_attn_dt_b"].reshape(NIG, 128).T)
    Dr = np.ascontiguousarray(inputs["L0_attn_D"].reshape(NIG, 128).T)
    alogr = np.ascontiguousarray(
        inputs["L0_attn_A_log"].reshape(NIG, 128, NS).transpose(1, 0, 2)
        .reshape(128, NIG * NS))

    nc = build_nc()
    in_maps = []
    for core in range(8):
        in_maps.append({
            "xT": np.ascontiguousarray(x[core].T),
            "ipwT": ipwT, "convT": convT, "dtwT": dtwT, "bcwT": bcwT,
            "outwT": outwT, "w1T": w1T, "w2T": w2T, "b1r": b1r, "b2r": b2r,
            "convbr": convbr, "dtbr": dtbr, "Dr": Dr, "alogr": alogr,
            "maskT": np.ascontiguousarray(maskval[core].T),
        })
    res = run_bass_kernel_spmd(nc, in_maps, core_ids=list(range(8)))
    out = np.empty((8, SEQ, DM), np.float32)
    for core in range(8):
        out[core] = res.results[core]["outT"].T
    return out


# version 26
# speedup vs baseline: 1.3672x; 1.0870x over previous
import os
import numpy as np
import ml_dtypes
import concourse.bass as bass
import concourse.mybir as mybir
from concourse.bass_utils import run_bass_kernel_spmd

F32 = mybir.dt.float32
BF16 = mybir.dt.bfloat16
AF = mybir.ActivationFunctionType
OP = mybir.AluOpType
NPBF = ml_dtypes.bfloat16

SEQ = 2048
DM = 1024
DI = 2048
DH = 4096
NS = 16
TC = 512
NCHUNK = SEQ // TC
NDG = DM // 128
NIG = DI // 128
NHG = DH // 128

LN2 = float(np.log(2.0))
F = lambda v: float(np.float32(v))


def _threefry_bits(n):
    idx = np.arange(n, dtype=np.uint64)
    x0 = (idx >> np.uint64(32)).astype(np.uint32)
    x1 = (idx & np.uint64(0xFFFFFFFF)).astype(np.uint32)
    rot = [np.uint32(r) for r in (13, 15, 26, 6, 17, 29, 16, 24)]
    ks0 = np.uint32(0)
    ks1 = np.uint32(42)
    ks2 = np.uint32(np.uint32(0x1BD11BDA) ^ ks0 ^ ks1)

    def rotl(x, r):
        return (x << r) | (x >> (np.uint32(32) - r))

    def rounds(x0, x1, rs):
        for r in rs:
            x0 = x0 + x1
            x1 = rotl(x1, r)
            x1 = x1 ^ x0
        return x0, x1

    with np.errstate(over="ignore"):
        x0 = x0 + ks0
        x1 = x1 + ks1
        x0, x1 = rounds(x0, x1, rot[:4])
        x0 = x0 + ks1
        x1 = x1 + ks2 + np.uint32(1)
        x0, x1 = rounds(x0, x1, rot[4:])
        x0 = x0 + ks2
        x1 = x1 + ks0 + np.uint32(2)
        x0, x1 = rounds(x0, x1, rot[:4])
        x0 = x0 + ks0
        x1 = x1 + ks1 + np.uint32(3)
        x0, x1 = rounds(x0, x1, rot[4:])
        x0 = x0 + ks1
        x1 = x1 + ks2 + np.uint32(4)
        x0, x1 = rounds(x0, x1, rot[:4])
        x0 = x0 + ks2
        x1 = x1 + ks0 + np.uint32(5)
    return x0, x1


def _keep_mask(fp64_mode):
    n = 8 * SEQ * DM
    b1, b2 = _threefry_bits(n)
    if fp64_mode:
        u64 = (b1.astype(np.uint64) << np.uint64(32)) | b2.astype(np.uint64)
        fb = (u64 >> np.uint64(12)) | np.uint64(0x3FF0000000000000)
        keep = (fb.view(np.float64) - 1.0) < 0.8
    else:
        bits = b1 ^ b2
        fb = (bits >> np.uint32(9)) | np.uint32(0x3F800000)
        keep = (fb.view(np.float32) - np.float32(1.0)) < np.float32(0.8)
    return keep.reshape(8, SEQ, DM)


class Ctr:

    def __init__(self, nc):
        self.sem = {}
        self.cnt = {}
        for name in ("pe", "act", "dve", "pool", "ld", "st"):
            self.sem[name] = nc.alloc_semaphore("sem_" + name)
            self.cnt[name] = 0

    def inc(self, inst, which, n=1):
        inst.then_inc(self.sem[which], n)
        self.cnt[which] += n

    def wait(self, eng, which, val=None):
        eng.wait_ge(self.sem[which], self.cnt[which] if val is None else val)


SAFE_ACT = True
SAFE_DVE = os.environ.get("KSAFE_DVE", "1") == "1"

_EMIT_OPS = {"activation", "tensor_scalar", "tensor_tensor", "tensor_copy",
             "reciprocal",
             "scalar_tensor_tensor", "tensor_tensor_scan", "memset",
             "tensor_scalar_add", "sign", "square", "copy", "mul", "add"}


class SafeEng:

    def __init__(self, eng, b, name, strict):
        self._e, self._b, self._n, self._strict = eng, b, name, strict

    def __getattr__(self, k):
        f = getattr(self._e, k)
        if k in _EMIT_OPS:
            def g(*a, **kw):
                if self._strict:
                    self._b.wait(self._e, self._n)
                inst = f(*a, **kw)
                self._b.inc(inst, self._n)
                return inst
            return g
        return f


def build_nc():
    nc = bass.Bass("TRN2", target_bir_lowering=False,
                   detect_race_conditions=False)
    DT = nc.dram_tensor

    xT = DT("xT", [DM, SEQ], F32, kind="ExternalInput")
    ipwT = DT("ipwT", [DM, 2 * DI], F32, kind="ExternalInput")
    convT = DT("convT", [4, DI, DI], F32, kind="ExternalInput")
    dtwT = DT("dtwT", [DI, DI], F32, kind="ExternalInput")
    bcwT = DT("bcwT", [DI, 32], F32, kind="ExternalInput")
    outwT = DT("outwT", [DI, DM], F32, kind="ExternalInput")
    w1T = DT("w1T", [DM, DH], F32, kind="ExternalInput")
    w2T = DT("w2T", [DH, DM], F32, kind="ExternalInput")
    b1r = DT("b1r", [128, NHG], F32, kind="ExternalInput")
    b2r = DT("b2r", [128, NDG], F32, kind="ExternalInput")
    convbr = DT("convbr", [128, NIG], F32, kind="ExternalInput")
    dtbr = DT("dtbr", [128, NIG], F32, kind="ExternalInput")
    Dr = DT("Dr", [128, NIG], F32, kind="ExternalInput")
    alogr = DT("alogr", [128, NIG * NS], F32, kind="ExternalInput")
    maskT = DT("maskT", [DM, SEQ], BF16, kind="ExternalInput")
    outT = DT("outT", [DM, SEQ], F32, kind="ExternalOutput")
    dbg = os.environ.get("KDBG_X2") == "1"
    if dbg:
        dbgT = DT("dbgT", [128, NDG * SEQ], BF16, kind="ExternalOutput")

    s_ipw = DT("s_ipw", [NDG, 128, 2 * DI], BF16, kind="Internal")
    s_conv = DT("s_conv", [NIG, 4, 128, DI], BF16, kind="Internal")
    s_dtw = DT("s_dtw", [NIG, 128, DI], BF16, kind="Internal")
    s_bcw = DT("s_bcw", [NIG, 128, 32], BF16, kind="Internal")
    s_outw = DT("s_outw", [NIG, 128, DM], BF16, kind="Internal")
    s_qw1 = DT("s_qw1", [NDG, 128, DH], BF16, kind="Internal")
    s_qw2 = DT("s_qw2", [NHG, 128, DM], BF16, kind="Internal")
    s_x2 = DT("s_x2", [128, NDG * SEQ], BF16, kind="Internal")

    import contextlib
    stack = contextlib.ExitStack()
    with stack:
        x2c16 = stack.enter_context(nc.sbuf_tensor("x2c16", [128, NDG * TC], BF16))
        carry = stack.enter_context(nc.sbuf_tensor("carry", [128, NIG * NS], F32))
        aneg = stack.enter_context(nc.sbuf_tensor("aneg", [128, NIG * NS], F32))
        dvec = stack.enter_context(nc.sbuf_tensor("dvec", [128, NIG], F32))
        convb = stack.enter_context(nc.sbuf_tensor("convb", [128, NIG], F32))
        dtb = stack.enter_context(nc.sbuf_tensor("dtb", [128, NIG], F32))
        qb1 = stack.enter_context(nc.sbuf_tensor("qb1", [128, NHG], F32))
        qb2 = stack.enter_context(nc.sbuf_tensor("qb2", [128, NDG], F32))
        cpool = stack.enter_context(nc.sbuf_tensor("cpool", [128, 16], F32))
        ones_c16 = stack.enter_context(nc.sbuf_tensor("ones_c16", [128, 1], BF16))
        ones_r16 = stack.enter_context(nc.sbuf_tensor("ones_r16", [1, 128], BF16))
        xch16 = stack.enter_context(nc.sbuf_tensor("xch16", [128, NDG * TC], BF16))
        ln1_16 = stack.enter_context(nc.sbuf_tensor("ln1_16", [128, NDG * TC], BF16))
        big16 = stack.enter_context(nc.sbuf_tensor("big16", [128, NIG * 515], BF16))
        zs16 = stack.enter_context(nc.sbuf_tensor("zs16", [128, NIG * TC], BF16))
        dt16 = stack.enter_context(nc.sbuf_tensor("dt16", [128, NIG * TC], BF16))
        u16 = stack.enter_context(nc.sbuf_tensor("u16", [128, NIG * TC], BF16))
        y32 = stack.enter_context(nc.sbuf_tensor("y32", [128, NIG * TC], F32))
        wtb = stack.enter_context(nc.sbuf_tensor("wtb", [128, 8192], BF16))
        wt16 = wtb
        scr32 = stack.enter_context(nc.sbuf_tensor("scr32", [128, 2048], F32))
        wst32 = scr32
        bbc16 = stack.enter_context(nc.sbuf_tensor("bbc16", [128, TC], BF16))
        cbc16 = stack.enter_context(nc.sbuf_tensor("cbc16", [128, TC], BF16))
        bm16 = stack.enter_context(nc.sbuf_tensor("bm16", [32, TC], BF16))
        brow16 = stack.enter_context(nc.sbuf_tensor("brow16", [1, TC], BF16))
        crow16 = stack.enter_context(nc.sbuf_tensor("crow16", [1, TC], BF16))
        brow2 = stack.enter_context(nc.sbuf_tensor("brow2", [1, TC], BF16))
        crow2 = stack.enter_context(nc.sbuf_tensor("crow2", [1, TC], BF16))
        bbc2 = stack.enter_context(nc.sbuf_tensor("bbc2", [128, TC], BF16))
        cbc2 = stack.enter_context(nc.sbuf_tensor("cbc2", [128, TC], BF16))
        sq16 = stack.enter_context(nc.sbuf_tensor("sq16", [128, TC], BF16))
        qa_t = stack.enter_context(nc.sbuf_tensor("qa", [128, TC], F32))
        qs_t = stack.enter_context(nc.sbuf_tensor("qs", [128, TC], F32))
        ql_t = stack.enter_context(nc.sbuf_tensor("ql", [128, TC], F32))
        qf_t = stack.enter_context(nc.sbuf_tensor("qf", [128, TC], F32))
        qt_t = stack.enter_context(nc.sbuf_tensor("qt", [128, TC], F32))
        qm_t = stack.enter_context(nc.sbuf_tensor("qm", [128, TC], F32))
        qsc_t = stack.enter_context(nc.sbuf_tensor("qsc", [128, TC], F32))
        qisc_t = stack.enter_context(nc.sbuf_tensor("qisc", [128, TC], F32))
        qa2_t = stack.enter_context(nc.sbuf_tensor("qa2", [128, TC], F32))
        qs2_t = stack.enter_context(nc.sbuf_tensor("qs2", [128, TC], F32))
        ql2_t = stack.enter_context(nc.sbuf_tensor("ql2", [128, TC], F32))
        qf2_t = stack.enter_context(nc.sbuf_tensor("qf2", [128, TC], F32))
        qt2_t = stack.enter_context(nc.sbuf_tensor("qt2", [128, TC], F32))
        qm2_t = stack.enter_context(nc.sbuf_tensor("qm2", [128, TC], F32))
        qsc2_t = stack.enter_context(nc.sbuf_tensor("qsc2", [128, TC], F32))
        qisc2_t = stack.enter_context(nc.sbuf_tensor("qisc2", [128, TC], F32))
        sdA = scr32[:, 0:TC]
        sdB = scr32[:, TC:2 * TC]
        sst = scr32[:, 2 * TC:3 * TC]
        stmp = scr32[:, 3 * TC:4 * TC]
        mrow = stack.enter_context(nc.sbuf_tensor("mrow", [1, TC], F32))
        trow = stack.enter_context(nc.sbuf_tensor("trow", [1, TC], F32))
        rstd = stack.enter_context(nc.sbuf_tensor("rstd", [1, TC], F32))
        brow = stack.enter_context(nc.sbuf_tensor("brow", [1, TC], F32))
        rsth, rstl, broh, brol = brow16, crow16, brow2, crow2
        ps0 = stack.enter_context(nc.psum_tensor("ps0", [128, TC], F32))
        ps1 = stack.enter_context(nc.psum_tensor("ps1", [128, TC], F32))
        ps2 = stack.enter_context(nc.psum_tensor("ps2", [128, TC], F32))
        ps3 = stack.enter_context(nc.psum_tensor("ps3", [128, TC], F32))
        ps4 = stack.enter_context(nc.psum_tensor("ps4", [128, TC], F32))
        ps5 = stack.enter_context(nc.psum_tensor("ps5", [128, TC], F32))
        ps6 = stack.enter_context(nc.psum_tensor("ps6", [128, TC], F32))
        ps7 = stack.enter_context(nc.psum_tensor("ps7", [128, TC], F32))
        ps = [ps0, ps1, ps2, ps3, ps4, ps5, ps6, ps7]
        b = Ctr(nc)

        class WBlk:
            def __init__(self, blk):
                self._blk = blk

            def vector(self, fn):
                self._blk.vector(lambda v: fn(SafeEng(v, b, "dve", SAFE_DVE)))

            def scalar(self, fn):
                self._blk.scalar(lambda s: fn(SafeEng(s, b, "act", SAFE_ACT)))

            def gpsimd(self, fn):
                self._blk.gpsimd(lambda g: fn(SafeEng(g, b, "pool", True)))

            def __getattr__(self, k):
                return getattr(self._blk, k)

        import contextlib as _ctx

        @_ctx.contextmanager
        def Block(**kw):
            with nc.Block(**kw) as blk:
                yield WBlk(blk)

        CONSTS = [LN2, -LN2, 9 * LN2, -9 * LN2, 17 * LN2, -17 * LN2,
                  24 * LN2, -24 * LN2, -8 * LN2, 1e-38, 1e-5, -0.5, 0.5,
                  1e-6, 1e-4, 1.0 / 1024.0]
        for j, cv in enumerate(CONSTS):
            nc.const_aps.register(F32, F(cv), cpool[:, j:j + 1])

        with Block(no_gpsimd_drain=True) as blk:
            def _init_v(v):
                for j, cv in enumerate(CONSTS):
                    v.memset(cpool[:, j:j + 1], F(cv))
                v.memset(ones_c16[:], 1.0)
                v.memset(ones_r16[:], 1.0)
                v.memset(carry[:], 0.0)
            blk.vector(_init_v)

            def _init_dma(sync):
                for dst, src in ((convb, convbr), (dtb, dtbr), (dvec, Dr),
                                 (aneg, alogr)):
                    b.inc(sync.dma_start(dst[:], src[:]), "ld", 16)
                b.wait(sync, "ld")
            blk.sync(_init_dma)

        with Block(no_gpsimd_drain=True) as blk:
            blk.scalar(lambda s: s.activation(aneg[:], aneg[:], AF.Exp))
            def _negv(v):
                b.wait(v, "act")
                v.tensor_scalar(aneg[:], aneg[:], -1.0, None, OP.mult)
            blk.vector(_negv)

        LSCR = [(qa_t, qs_t, ql_t, qf_t, qt_t, qm_t, qsc_t, qisc_t),
                (qa2_t, qs2_t, ql2_t, qf2_t, qt2_t, qm2_t, qsc2_t, qisc2_t)]
        lad_state = {"ctr": 0, 0: 0, 1: 0}
        def ladder_prefix(blk, src_ap, W, bias_col, min_e_f, top, par,
                          src_dve=0):
            qaP, qsP, qlP, qfP, qtP = LSCR[par][0:5]
            A_, S_, L_, F_, T_ = (qaP[:, 0:W], qsP[:, 0:W], qlP[:, 0:W],
                                  qfP[:, 0:W], qtP[:, 0:W])
            bias = 0.0 if bias_col is None else bias_col

            def s1(s):
                b.wait(s, "dve", max(lad_state[par], src_dve))
                s.activation(A_, src_ap, AF.Abs, bias=bias)
                s.activation(S_, src_ap, AF.Sign, bias=bias)
                b.wait(s, "act")
                s.activation(L_, A_, AF.Ln, bias=F(1e-38))
            blk.scalar(s1)

            def v1(v):
                b.wait(v, "act")
                v.tensor_scalar(L_, L_, F(1 / LN2), F(min_e_f), OP.mult, OP.max)
                step0 = float(top // 2)
                v.tensor_scalar(F_, L_, F(step0 + min_e_f), None, OP.is_ge)
                v.tensor_scalar(F_, F_, step0, None, OP.mult)
                step = step0 / 2
                while step >= 1.0:
                    v.tensor_scalar(T_, F_, F(step + min_e_f), None, OP.add)
                    v.tensor_tensor(T_, L_, T_, OP.is_ge)
                    v.scalar_tensor_tensor(F_, T_, step, F_, OP.mult, OP.add)
                    step /= 2
            blk.vector(v1)

        def ladder_round(blk, W, levels, par, thr_ap=None, gate_relu=False,
                         out16=None, out32=None):
            qaP, qsP, qlP, qfP, qtP, qmP, qscP, qiscP = LSCR[par]
            A_, S_, F_, T_, M_ = (qaP[:, 0:W], qsP[:, 0:W], qfP[:, 0:W],
                                  qtP[:, 0:W], qmP[:, 0:W])
            G_ = T_
            SC, IS = qscP[:, 0:W], qiscP[:, 0:W]
            V_ = A_

            def v2(v):
                b.wait(v, "act")
                v.tensor_tensor(V_, A_, IS, OP.mult)
                v.tensor_scalar(V_, V_, 0.5, None, OP.add)
                step0 = float(levels[0])
                v.tensor_scalar(F_, V_, step0, None, OP.is_ge)
                v.tensor_scalar(F_, F_, step0, None, OP.mult)
                for step in levels[1:]:
                    v.tensor_scalar(T_, F_, float(step), None, OP.add)
                    v.tensor_tensor(T_, V_, T_, OP.is_ge)
                    v.scalar_tensor_tensor(F_, T_, float(step), F_,
                                           OP.mult, OP.add)
                v.tensor_tensor(M_, V_, F_, OP.subtract)
                next_ = 2 if levels[0] > 8 else 1
                for _ in range(next_):
                    v.tensor_scalar(T_, M_, 2.0, None, OP.is_ge)
                    v.scalar_tensor_tensor(F_, T_, 2.0, F_, OP.mult, OP.add)
                    v.scalar_tensor_tensor(M_, T_, -2.0, M_, OP.mult, OP.add)
                if thr_ap is None:
                    v.tensor_scalar(G_, M_, F(1.0 + 2.0 ** -30), None,
                                    OP.is_gt)
                else:
                    v.tensor_tensor(G_, M_, thr_ap, OP.is_gt)
                v.tensor_tensor(F_, F_, G_, OP.add)
                v.tensor_tensor(F_, F_, SC, OP.mult)
                if gate_relu:
                    v.tensor_scalar(S_, S_, 0.0, None, OP.max)
                last = None
                if out16 is not None:
                    last = v.tensor_tensor(out16, F_, S_, OP.mult)
                if out32 is not None:
                    last = v.tensor_tensor(out32, F_, S_, OP.mult)
                last
            blk.vector(v2)

        def emit_q43(blk, src_ap, W, bias_col=None, out16=None, out32=None,
                     src_dve=0):
            par = lad_state["ctr"] % 2
            lad_state["ctr"] += 1
            qfP, qscP, qiscP = LSCR[par][3], LSCR[par][6], LSCR[par][7]
            ladder_prefix(blk, src_ap, W, bias_col, -6.0, 16, par,
                          src_dve=src_dve)

            def s2(s):
                b.wait(s, "dve")
                s.activation(qiscP[:, 0:W], qfP[:, 0:W], AF.Exp,
                             bias=F(9 * LN2), scale=F(-LN2))
                s.activation(qscP[:, 0:W], qfP[:, 0:W], AF.Exp,
                             bias=F(-9 * LN2), scale=F(LN2))
            blk.scalar(s2)
            ladder_round(blk, W, [8, 4, 2], par, out16=out16, out32=out32)
            lad_state[par] = b.cnt["dve"]

        def emit_qh(blk, src_ap, W, bias_col, out16):
            par = lad_state["ctr"] % 2
            lad_state["ctr"] += 1
            qlP, qfP, qmP, qscP, qiscP = (LSCR[par][2], LSCR[par][3],
                                          LSCR[par][5], LSCR[par][6],
                                          LSCR[par][7])
            ladder_prefix(blk, src_ap, W, bias_col, -14.0, 32, par)

            def v15(v):
                v.tensor_scalar(qmP[:, 0:W], qfP[:, 0:W], 8.0, None, OP.max)
                v.tensor_tensor(qlP[:, 0:W], qfP[:, 0:W], qmP[:, 0:W],
                                OP.subtract)
                v.tensor_copy(qfP[:, 0:W], qmP[:, 0:W])
            blk.vector(v15)

            def s2(s):
                b.wait(s, "dve")
                s.activation(qiscP[:, 0:W], qfP[:, 0:W], AF.Exp,
                             bias=F(17 * LN2), scale=F(-LN2))
                s.activation(qscP[:, 0:W], qfP[:, 0:W], AF.Exp,
                             bias=F(-17 * LN2), scale=F(LN2))
                s.activation(qlP[:, 0:W], qlP[:, 0:W], AF.Exp,
                             bias=F(-8 * LN2), scale=F(LN2))
            blk.scalar(s2)

            def v16(v):
                b.wait(v, "act")
                v.tensor_scalar(qlP[:, 0:W], qlP[:, 0:W], 1.0, None, OP.add)
            blk.vector(v16)
            ladder_round(blk, W, [8, 4, 2], par, thr_ap=qlP[:, 0:W],
                         gate_relu=True, out16=out16)
            lad_state[par] = b.cnt["dve"]

        def emit_q510(blk, src_ap, W, bias_col, out32):
            par = lad_state["ctr"] % 2
            lad_state["ctr"] += 1
            qfP, qscP, qiscP = LSCR[par][3], LSCR[par][6], LSCR[par][7]
            ladder_prefix(blk, src_ap, W, bias_col, -14.0, 32, par)

            def s2(s):
                b.wait(s, "dve")
                s.activation(qiscP[:, 0:W], qfP[:, 0:W], AF.Exp,
                             bias=F(24 * LN2), scale=F(-LN2))
                s.activation(qscP[:, 0:W], qfP[:, 0:W], AF.Exp,
                             bias=F(-24 * LN2), scale=F(LN2))
            blk.scalar(s2)
            ladder_round(blk, W, [1024, 512, 256, 128, 64, 32, 16, 8, 4, 2],
                         par, out32=out32)
            lad_state[par] = b.cnt["dve"]

        prep = []
        for g in range(NDG):
            prep.append(("cast", ipwT[g * 128:(g + 1) * 128, :], s_ipw[g],
                         2 * DI))
        for ig in range(NIG):
            for k in range(4):
                prep.append(("cast", convT[k, ig * 128:(ig + 1) * 128, :],
                             s_conv[ig, k], DI))
        for g in range(NIG):
            prep.append(("cast", dtwT[g * 128:(g + 1) * 128, :], s_dtw[g], DI))
            prep.append(("cast", bcwT[g * 128:(g + 1) * 128, :], s_bcw[g], 32))
            prep.append(("cast", outwT[g * 128:(g + 1) * 128, :], s_outw[g],
                         DM))
        for g in range(NDG):
            prep.append(("q43", w1T[g * 128:(g + 1) * 128, :], s_qw1[g], DH))
        for g in range(NHG):
            prep.append(("q43", w2T[g * 128:(g + 1) * 128, :], s_qw2[g], DM))

        chunks = []
        for kind, src_, dst, width in prep:
            for j in range(0, width, 2048):
                chunks.append((kind, src_, dst, width, j, min(2048, width - j)))
        ld_hist, dve_hist, st_hist = [], [], []
        with Block(no_gpsimd_drain=True) as blk:
            for idx, (kind, src_, dst, width, j, wdt) in enumerate(chunks):
                stage = y32[:, (idx % 4) * 2048:(idx % 4) * 2048 + wdt]
                outb = wtb[:, (idx % 2) * 2048:(idx % 2) * 2048 + wdt]

                def _ld(sync, src_=src_, j=j, wdt=wdt, stage=stage, idx=idx):
                    if idx >= 4:
                        b.wait(sync, "dve", dve_hist[idx - 4])
                    b.inc(sync.dma_start(stage, src_[:, j:j + wdt]), "ld", 16)
                blk.sync(_ld)
                ld_hist.append(b.cnt["ld"])

                if kind == "cast":
                    def _cv(v, stage=stage, outb=outb, idx=idx):
                        v.wait_ge(b.sem["ld"], ld_hist[idx])
                        if idx >= 2:
                            v.wait_ge(b.sem["st"], st_hist[idx - 2])
                        v.tensor_copy(outb, stage)
                    blk.vector(_cv)
                else:
                    def _w(s, idx=idx):
                        s.wait_ge(b.sem["ld"], ld_hist[idx])
                        if idx >= 2:
                            s.wait_ge(b.sem["st"], st_hist[idx - 2])
                    blk.scalar(_w)

                    def _wv(v, idx=idx):
                        if idx >= 2:
                            v.wait_ge(b.sem["st"], st_hist[idx - 2])
                    blk.vector(_wv)
                    for q0 in range(0, wdt, TC):
                        qw = min(TC, wdt - q0)
                        emit_q43(blk, stage[:, q0:q0 + qw], qw,
                                 out16=outb[:, q0:q0 + qw])
                dve_hist.append(b.cnt["dve"])

                def _st(g_, dst=dst, j=j, wdt=wdt, outb=outb, idx=idx):
                    b.wait(g_, "dve", dve_hist[idx])
                    b.inc(g_.dma_start(dst[:, j:j + wdt], outb), "st", 16)
                blk.gpsimd(_st)
                st_hist.append(b.cnt["st"])

        with Block(no_gpsimd_drain=True) as blk:
            def _ldb(sync):
                b.wait(sync, "st")
                b.inc(sync.dma_start(wst32[:, 0:NHG], b1r[:]), "ld", 16)
                b.inc(sync.dma_start(wst32[:, NHG:NHG + NDG], b2r[:]),
                      "ld", 16)
            blk.sync(_ldb)
            blk.scalar(lambda s: b.wait(s, "ld"))
            emit_q43(blk, wst32[:, 0:NHG + NDG], NHG + NDG,
                     out32=y32[:, 0:NHG + NDG])
            def _cpb(v):
                v.tensor_copy(qb1[:], y32[:, 0:NHG])
                v.tensor_copy(qb2[:], y32[:, NHG:NHG + NDG])
            blk.vector(_cpb)

        with Block(no_gpsimd_drain=True) as blk:
            blk.gpsimd(lambda g_: b.wait(g_, "st"))

        def ln_stats(slices16):
            n = 128 * len(slices16)
            with Block(no_gpsimd_drain=True) as blk:
                for i, sl in enumerate(slices16):
                    first, last = i == 0, i == len(slices16) - 1

                    def _sq(s, sl=sl, first=first):
                        if not first:
                            b.wait(s, "pe", b.cnt["pe"])
                        s.activation(sq16[:], sl, AF.Square)
                    blk.scalar(_sq)

                    def _mm(t, sl=sl, first=first, last=last):
                        t.matmul(ps6[0:1, :], ones_c16[:], sl,
                                 start=first, stop=last)
                        b.wait(t, "act")
                        b.inc(t.matmul(ps7[0:1, :], ones_c16[:], sq16[:],
                                       start=first, stop=last), "pe")
                    blk.tensor(_mm)

                def _stat_v(v):
                    b.wait(v, "pe")
                    v.tensor_scalar(mrow[:], ps6[0:1, :], F(1.0 / n), None,
                                    OP.mult)
                    v.tensor_scalar(trow[:], ps7[0:1, :], F(1.0 / n), None,
                                    OP.mult)
                    v.tensor_tensor(rstd[:], mrow[:], mrow[:], OP.mult)
                    v.tensor_tensor(trow[:], trow[:], rstd[:],
                                          OP.subtract)
                blk.vector(_stat_v)

                def _stat_s(s):
                    b.wait(s, "dve")
                    s.activation(trow[:], trow[:], AF.Ln, bias=F(1e-5))
                    b.wait(s, "act")
                    s.activation(rstd[:], trow[:], AF.Exp, scale=-0.5)
                blk.scalar(_stat_s)

                def _stat_v2(v):
                    b.wait(v, "act")
                    v.tensor_tensor(brow[:], mrow[:], rstd[:], OP.mult)
                    v.tensor_scalar(brow[:], brow[:], -1.0, None, OP.mult)
                    v.tensor_copy(rsth[:], rstd[:])
                    v.tensor_tensor(trow[:], rstd[:], rsth[:], OP.subtract)
                    v.tensor_copy(rstl[:], trow[:])
                    v.tensor_copy(broh[:], brow[:])
                    v.tensor_tensor(trow[:], brow[:], broh[:], OP.subtract)
                    v.tensor_copy(brol[:], trow[:])
                blk.vector(_stat_v2)

                def _bc(t):
                    b.wait(t, "dve")
                    t.matmul(ps6[:], ones_r16[:], rsth[:], start=True,
                             stop=False)
                    t.matmul(ps6[:], ones_r16[:], rstl[:], start=False,
                             stop=True)
                    t.matmul(ps7[:], ones_r16[:], broh[:], start=True,
                             stop=False)
                    b.inc(t.matmul(ps7[:], ones_r16[:], brol[:], start=False,
                                   stop=True), "pe")
                blk.tensor(_bc)

        def mm_phase(n_og, kgs, load_fn, rhs_fn, evac_fn, ogblk=8):
            nk = len(kgs)
            for ogb in range(0, n_og, ogblk):
                nb = min(ogblk, n_og - ogb)
                KB = max(1, 4096 // (nb * 128))
                half_w = KB * nb * 128
                pe_hist = []
                with Block(no_gpsimd_drain=True) as blk:
                    for bi in range(0, nk, KB):
                        kgrp = kgs[bi:bi + KB]
                        par = (bi // KB) % 2
                        base = par * half_w

                        def _ld(sync, kgrp=kgrp, base=base, bi=bi, nb=nb,
                                ogb=ogb, pe_hist=tuple(pe_hist)):
                            if len(pe_hist) >= 2:
                                b.wait(sync, "pe", pe_hist[-2])
                            for j, kg in enumerate(kgrp):
                                load_fn(sync, kg, ogb, nb,
                                        base + j * nb * 128)
                        blk.sync(_ld)

                        def _mm(t, kgrp=kgrp, base=base, bi=bi, nb=nb, nk=nk):
                            b.wait(t, "ld")
                            for j, kg in enumerate(kgrp):
                                for oi in range(nb):
                                    off = base + j * nb * 128 + oi * 128
                                    mm = t.matmul(
                                        ps[oi][:],
                                        wtb[:, off:off + 128],
                                        rhs_fn(kg),
                                        start=(bi + j == 0),
                                        stop=(bi + j == nk - 1))
                            b.inc(mm, "pe")
                        blk.tensor(_mm)
                        pe_hist.append(b.cnt["pe"])

                    def _ev(s, ogb=ogb, nb=nb):
                        b.wait(s, "pe")
                        for oi in range(nb):
                            evac_fn(s, ogb + oi, ps[oi])
                    blk.scalar(_ev)
                    yield blk, ogb, nb

        def run_mm(gen):
            for _ in gen:
                pass

        def mk_loader(scr, colw):
            def load(sync, kg, ogb, nb, off):
                b.inc(sync.dma_start(
                    wtb[:, off:off + nb * 128],
                    scr[kg][:, ogb * 128:ogb * 128 + nb * 128]), "ld", 16)
            return load

        for c in range(NCHUNK):
            t0 = c * TC

            with Block(no_gpsimd_drain=True) as blk:
                for dg in range(NDG):
                    def _lx(sync, dg=dg):
                        if dg > 0:
                            b.wait(sync, "dve", b.cnt["dve"])
                        b.inc(sync.dma_start(
                            sst, xT[dg * 128:(dg + 1) * 128, t0:t0 + TC]),
                            "ld", 16)
                    blk.sync(_lx)

                    def _cx(v, dg=dg):
                        b.wait(v, "ld")
                        v.tensor_copy(xch16[:, dg * TC:(dg + 1) * TC],
                                      sst)
                    blk.vector(_cx)

                def _pfx(g_, c=c):
                    for g in range(NIG):
                        if c == 0:
                            g_.memset(big16[:, g * 515:g * 515 + 3], 0.0)
                        else:
                            g_.tensor_copy(
                                big16[:, g * 515:g * 515 + 3],
                                big16[:, g * 515 + 512:g * 515 + 515])
                blk.gpsimd(_pfx)

            ln_stats([xch16[:, dg * TC:(dg + 1) * TC] for dg in range(NDG)])
            with Block(no_gpsimd_drain=True) as blk:
                def _ap1(v):
                    b.wait(v, "pe")
                    for dg in range(NDG):
                        sl = xch16[:, dg * TC:(dg + 1) * TC]
                        ot = ln1_16[:, dg * TC:(dg + 1) * TC]
                        v.tensor_tensor(stmp, sl, ps6[:], OP.mult)
                        v.tensor_tensor(ot, stmp, ps7[:], OP.add)
                blk.vector(_ap1)

            def ipw_evac(s, og, psum, c=c):
                if og < NIG:
                    b.wait(s, "pool", b.cnt["pool"])
                    dst = big16[:, og * 515 + 3:og * 515 + 515]
                    s.activation(dst, psum[:], AF.Copy)
                else:
                    g = og - NIG
                    s.activation(zs16[:, g * TC:(g + 1) * TC], psum[:],
                                 AF.Exp, scale=-1.0)
                    zsev.append((g, psum))

            zsev = []
            for _blk, _ogb, _nb in mm_phase(
                    2 * NIG, list(range(NDG)), mk_loader(s_ipw, None),
                    lambda kg: ln1_16[:, kg * TC:(kg + 1) * TC], ipw_evac):
                if zsev:
                    def _zfin(v, pairs=tuple(zsev)):
                        b.wait(v, "act")
                        for g, psum in pairs:
                            sl = zs16[:, g * TC:(g + 1) * TC]
                            v.tensor_scalar(stmp, sl, 1.0, None, OP.add)
                            v.reciprocal(stmp, stmp)
                            v.tensor_tensor(sl, stmp, psum[:], OP.mult)
                    _blk.vector(_zfin)
                    zsev = []

            def conv_load(sync, igk, ogb, nb, off):
                ig, k = igk
                b.inc(sync.dma_start(
                    wtb[:, off:off + nb * 128],
                    s_conv[ig, k][:, ogb * 128:ogb * 128 + nb * 128]),
                    "ld", 16)

            def conv_evac(s, og, psum):
                s.activation(u16[:, og * TC:(og + 1) * TC], psum[:],
                                   AF.Identity, bias=convb[:, og:og + 1])

            run_mm(mm_phase(NIG, [(ig, k) for ig in range(NIG)
                                  for k in range(4)],
                            conv_load,
                            lambda igk: big16[:, igk[0] * 515 + igk[1]:
                                              igk[0] * 515 + igk[1] + TC],
                            conv_evac))

            def dt_evac(s, og, psum):
                s.activation(dt16[:, og * TC:(og + 1) * TC], psum[:],
                             AF.Exp, scale=-1.0, bias=dtb[:, og:og + 1])

            for _blk, _ogb, _nb in mm_phase(
                    NIG, list(range(NIG)), mk_loader(s_dtw, None),
                    lambda kg: u16[:, kg * TC:(kg + 1) * TC], dt_evac):
                def _fin(v, ogs=tuple(range(_ogb, _ogb + _nb))):
                    b.wait(v, "act")
                    for og in ogs:
                        sl = dt16[:, og * TC:(og + 1) * TC]
                        v.tensor_scalar(stmp, sl, 1.0, None, OP.add)
                        v.reciprocal(sst, stmp)
                        v.tensor_copy(sl, sst)
                _blk.vector(_fin)

            with Block(no_gpsimd_drain=True) as blk:
                for kg in range(NIG):
                    def _lbc(sync, kg=kg):
                        if kg >= 1:
                            b.wait(sync, "pe", b.cnt["pe"])
                        b.inc(sync.dma_start(
                            wt16[:, (kg % 2) * 1024:(kg % 2) * 1024 + 32],
                            s_bcw[kg][:]), "ld", 16)
                    blk.sync(_lbc)

                    def _mbc(t, kg=kg):
                        b.wait(t, "ld")
                        b.inc(t.matmul(ps0[0:32, :],
                                       wt16[:, (kg % 2) * 1024:
                                            (kg % 2) * 1024 + 32],
                                       u16[:, kg * TC:(kg + 1) * TC],
                                       start=(kg == 0), stop=(kg == NIG - 1)),
                              "pe")
                    blk.tensor(_mbc)

                def _ebc(s):
                    b.wait(s, "pe")
                    s.activation(bm16[:], ps0[0:32, :], AF.Copy)
                blk.scalar(_ebc)

            with Block(no_gpsimd_drain=True) as blk:
                for g in range(NIG):
                    def _yi(s, g=g):
                        b.wait(s, "dve", b.cnt["dve"])
                        s.activation(y32[:, g * TC:(g + 1) * TC],
                                           u16[:, g * TC:(g + 1) * TC],
                                           AF.Identity,
                                           scale=dvec[:, g:g + 1])
                    blk.scalar(_yi)

                    def _du(v, g=g):
                        dsl = dt16[:, g * TC:(g + 1) * TC]
                        usl = u16[:, g * TC:(g + 1) * TC]
                        v.tensor_scalar(stmp, dsl, 1e-4, None, OP.max)
                        v.tensor_copy(dsl, stmp)
                        v.tensor_tensor(sdB, stmp, usl, OP.mult)
                        b.wait(v, "act")
                        v.tensor_copy(usl, sdB)
                    blk.vector(_du)
            du16 = u16

            evac_hist = {}
            gloop_hist = {}
            bcast_hist = {}
            with Block(no_gpsimd_drain=True) as blk:
                for si in range(NS):
                    par = si % 2
                    psB, psC = (ps4, ps5) if par == 0 else (ps6, ps7)
                    brw, crw = (brow16, crow16) if par == 0 else (brow2, crow2)
                    bbb, ccc = (bbc16, cbc16) if par == 0 else (bbc2, cbc2)

                    def _rows(sync, si=si, brw=brw, crw=crw):
                        if si >= 2:
                            b.wait(sync, "pe", bcast_hist[si - 2])
                        b.inc(sync.dma_start(brw[:], bm16[si:si + 1, :]),
                              "ld", 16)
                        b.inc(sync.dma_start(crw[:],
                                             bm16[16 + si:17 + si, :]),
                              "ld", 16)
                    blk.sync(_rows)

                    def _bc2(t, si=si, psB=psB, psC=psC, brw=brw, crw=crw):
                        b.wait(t, "ld")
                        if si >= 2:
                            b.wait(t, "act", evac_hist[si - 2])
                        t.matmul(psB[:], ones_r16[:], brw[:],
                                 start=True, stop=True)
                        b.inc(t.matmul(psC[:], ones_r16[:], crw[:],
                                       start=True, stop=True), "pe")
                    blk.tensor(_bc2)
                    bcast_hist[si] = b.cnt["pe"]

                    def _ebc2(s, si=si, psB=psB, psC=psC, bbb=bbb, ccc=ccc):
                        b.wait(s, "pe", bcast_hist[si])
                        if si >= 2:
                            b.wait(s, "dve", gloop_hist[si - 2])
                        s.activation(bbb[:], psB[:], AF.Copy)
                        s.activation(ccc[:], psC[:], AF.Copy)
                    blk.scalar(_ebc2)
                    evac_hist[si] = b.cnt["act"]

                    for g in range(NIG):
                        col = g * NS + si

                        def _dA(s, g=g, col=col):
                            b.wait(s, "dve", b.cnt["dve"])
                            s.activation(
                                sdA, dt16[:, g * TC:(g + 1) * TC], AF.Exp,
                                scale=aneg[:, col:col + 1])
                        blk.scalar(_dA)

                        def _sc(v, g=g, col=col, si=si, bbb=bbb, ccc=ccc):
                            if g == 0:
                                v.wait_ge(b.sem["act"], evac_hist[si])
                            v.tensor_tensor(sdB,
                                            du16[:, g * TC:(g + 1) * TC],
                                            bbb[:], OP.mult)
                            v.tensor_scalar(sdB, sdB, F(1e-6), None,
                                            OP.add)
                            b.wait(v, "act")
                            v.tensor_tensor_scan(sst, sdA, sdB,
                                                 carry[:, col:col + 1],
                                                 OP.mult, OP.add)
                            v.tensor_copy(carry[:, col:col + 1],
                                          sst[:, TC - 1:TC])
                            v.tensor_tensor(stmp, sst, ccc[:],
                                            OP.mult)
                            v.tensor_tensor(
                                y32[:, g * TC:(g + 1) * TC],
                                y32[:, g * TC:(g + 1) * TC], stmp, OP.add)
                        blk.vector(_sc)
                    gloop_hist[si] = b.cnt["dve"]

            with Block(no_gpsimd_drain=True) as blk:
                def _gt(v):
                    for g in range(NIG):
                        v.tensor_tensor(zs16[:, g * TC:(g + 1) * TC],
                                              y32[:, g * TC:(g + 1) * TC],
                                              zs16[:, g * TC:(g + 1) * TC],
                                              OP.mult)
                blk.vector(_gt)

            def attn_evac(s, og, psum):
                s.activation(ln1_16[:, og * TC:(og + 1) * TC], psum[:],
                                   AF.Copy)

            run_mm(mm_phase(NDG, list(range(NIG)), mk_loader(s_outw, None),
                            lambda kg: zs16[:, kg * TC:(kg + 1) * TC],
                            attn_evac))
            attn16 = ln1_16

            ln_stats([attn16[:, dg * TC:(dg + 1) * TC] for dg in range(NDG)])
            with Block(no_gpsimd_drain=True) as blk:
                for dg in range(NDG):
                    def _x2(v, dg=dg):
                        if dg == 0:
                            b.wait(v, "pe")
                        else:
                            b.wait(v, "st", b.cnt["st"])
                        asl = attn16[:, dg * TC:(dg + 1) * TC]
                        xsl = xch16[:, dg * TC:(dg + 1) * TC]
                        v.tensor_tensor(stmp, asl, ps6[:], OP.mult)
                        v.tensor_tensor(stmp, stmp, ps7[:], OP.add)
                        v.tensor_tensor(sq16[:], stmp, xsl, OP.add)
                    blk.vector(_x2)

                    def _x2s(g_, dg=dg, t0=t0):
                        b.wait(g_, "dve")
                        b.inc(g_.dma_start(
                            s_x2[:, dg * SEQ + t0:dg * SEQ + t0 + TC],
                            sq16[:]), "st", 16)
                    blk.gpsimd(_x2s)

        if dbg:
            with Block(no_gpsimd_drain=True) as blk:
                def _dbg(g_):
                    b.inc(g_.dma_start(dbgT[:], x2_16[:]), "st", 16)
                    b.wait(g_, "st")
                blk.gpsimd(_dbg)

        def qh_sl(og):
            if og < NIG:
                return dt16[:, og * TC:(og + 1) * TC]
            return u16[:, (og - NIG) * TC:(og - NIG + 1) * TC]

        for c in range(NCHUNK):
            t0 = c * TC
            x2sl = [x2c16[:, dg * TC:(dg + 1) * TC] for dg in range(NDG)]

            with Block(no_gpsimd_drain=True) as blk:
                def _lx2(sync, t0=t0):
                    b.wait(sync, "dve", b.cnt["dve"])
                    b.wait(sync, "st")
                    for dg in range(NDG):
                        b.inc(sync.dma_start(
                            x2c16[:, dg * TC:(dg + 1) * TC],
                            s_x2[:, dg * SEQ + t0:dg * SEQ + t0 + TC]),
                            "ld", 16)
                    b.wait(sync, "ld")
                blk.sync(_lx2)

            ln_stats(x2sl)
            for dg in range(NDG):
                with Block(no_gpsimd_drain=True) as blk:
                    def _l3(v, dg=dg):
                        b.wait(v, "pe")
                        v.tensor_tensor(stmp, x2sl[dg], ps6[:], OP.mult)
                        v.tensor_tensor(stmp, stmp, ps7[:],
                                              OP.add)
                    blk.vector(_l3)
                    emit_q43(blk, stmp, TC,
                             out16=xch16[:, dg * TC:(dg + 1) * TC],
                             src_dve=b.cnt["dve"])

            for ogb in range(0, NHG, 8):
                with Block(no_gpsimd_drain=True) as blk:
                    def _l1(sync, ogb=ogb):
                        for ki in range(NDG):
                            b.inc(sync.dma_start(
                                wtb[:, ki * 1024:(ki + 1) * 1024],
                                s_qw1[ki][:, ogb * 128:(ogb + 8) * 128]),
                                "ld", 16)
                    blk.sync(_l1)

                    def _m1(t):
                        b.wait(t, "ld")
                        for ki in range(NDG):
                            for oi in range(8):
                                off = ki * 1024 + oi * 128
                                mm = t.matmul(
                                    ps[oi][:],
                                    wtb[:, off:off + 128],
                                    xch16[:, ki * TC:(ki + 1) * TC],
                                    start=(ki == 0), stop=(ki == NDG - 1))
                        b.inc(mm, "pe")
                    blk.tensor(_m1)
                    blk.scalar(lambda s: b.wait(s, "pe"))
                    for oi in range(8):
                        og = ogb + oi
                        emit_qh(blk, ps[oi][:], TC, qb1[:, og:og + 1],
                                qh_sl(og))

            with Block(no_gpsimd_drain=True) as blk:
                pe_hist2 = []
                for bi in range(0, NHG, 4):
                    par = (bi // 4) % 2
                    base = par * 4096

                    def _l2(sync, bi=bi, base=base, pe_hist2=tuple(pe_hist2)):
                        if len(pe_hist2) >= 2:
                            b.wait(sync, "pe", pe_hist2[-2])
                        for j in range(4):
                            b.inc(sync.dma_start(
                                wtb[:, base + j * 1024:base + (j + 1) * 1024],
                                s_qw2[bi + j][:]), "ld", 16)
                    blk.sync(_l2)

                    def _m2(t, bi=bi, base=base):
                        b.wait(t, "ld")
                        for j in range(4):
                            for oi in range(NDG):
                                off = base + j * 1024 + oi * 128
                                mm = t.matmul(
                                    ps[oi][:],
                                    wtb[:, off:off + 128],
                                    qh_sl(bi + j),
                                    start=(bi + j == 0),
                                    stop=(bi + j == NHG - 1))
                        b.inc(mm, "pe")
                    blk.tensor(_m2)
                    pe_hist2.append(b.cnt["pe"])
                blk.scalar(lambda s: b.wait(s, "pe"))
                for dg in range(NDG):
                    emit_q510(blk, ps[dg][:], TC, qb2[:, dg:dg + 1],
                              y32[:, dg * TC:(dg + 1) * TC])

            with Block(no_gpsimd_drain=True) as blk:
                for dg in range(NDG):
                    def _lm(sync, dg=dg, t0=t0):
                        if dg > 0:
                            b.wait(sync, "dve", b.cnt["dve"])
                        b.inc(sync.dma_start(
                            sq16[:],
                            maskT[dg * 128:(dg + 1) * 128, t0:t0 + TC]),
                            "ld", 16)
                    blk.sync(_lm)

                    def _vm(v, dg=dg):
                        b.wait(v, "ld")
                        ysl = y32[:, dg * TC:(dg + 1) * TC]
                        v.tensor_tensor(ysl, ysl, sq16[:], OP.mult)
                        v.tensor_copy(ln1_16[:, dg * TC:(dg + 1) * TC],
                                            ysl)
                    blk.vector(_vm)

            ln_stats([ln1_16[:, dg * TC:(dg + 1) * TC] for dg in range(NDG)])
            with Block(no_gpsimd_drain=True) as blk:
                for dg in range(NDG):
                    def _o(v, dg=dg):
                        if dg == 0:
                            b.wait(v, "pe")
                        else:
                            b.wait(v, "st", b.cnt["st"])
                        ysl = y32[:, dg * TC:(dg + 1) * TC]
                        v.tensor_tensor(stmp, ysl, ps6[:], OP.mult)
                        v.tensor_tensor(stmp, stmp, ps7[:], OP.add)
                        v.tensor_tensor(sst, stmp, x2sl[dg],
                                              OP.add)
                    blk.vector(_o)

                    def _so(g_, dg=dg, t0=t0):
                        b.wait(g_, "dve")
                        b.inc(g_.dma_start(
                            outT[dg * 128:(dg + 1) * 128, t0:t0 + TC],
                            sst), "st", 16)
                    blk.gpsimd(_so)

        with Block(no_gpsimd_drain=True) as blk:
            blk.gpsimd(lambda g_: b.wait(g_, "st"))

    return nc


def kernel(**inputs):
    fp64_mode = os.environ.get("BASS_SIM_FP64") == "1"
    keep = _keep_mask(fp64_mode)
    maskval = np.where(keep, np.float32(1.25), np.float32(0.0)).astype(NPBF)

    x = np.ascontiguousarray(inputs["L0_attn_x"])
    ipwT = np.ascontiguousarray(inputs["L0_attn_in_proj_w"].T)
    convT = np.ascontiguousarray(
        np.transpose(inputs["L0_attn_conv_w"], (2, 1, 0)))
    dtwT = np.ascontiguousarray(inputs["L0_attn_dt_w"].T)
    bcwT = np.ascontiguousarray(
        np.concatenate([inputs["L0_attn_B_w"].T, inputs["L0_attn_C_w"].T],
                       axis=1))
    outwT = np.ascontiguousarray(inputs["L0_attn_out_w"].T)
    w1T = np.ascontiguousarray(inputs["L0_ffn_w1"].T)
    w2T = np.ascontiguousarray(inputs["L0_ffn_w2"].T)
    b1r = np.ascontiguousarray(inputs["L0_ffn_b1"].reshape(NHG, 128).T)
    b2r = np.ascontiguousarray(inputs["L0_ffn_b2"].reshape(NDG, 128).T)
    convbr = np.ascontiguousarray(
        inputs["L0_attn_conv_b"].reshape(NIG, 128).T)
    dtbr = np.ascontiguousarray(inputs["L0_attn_dt_b"].reshape(NIG, 128).T)
    Dr = np.ascontiguousarray(inputs["L0_attn_D"].reshape(NIG, 128).T)
    alogr = np.ascontiguousarray(
        inputs["L0_attn_A_log"].reshape(NIG, 128, NS).transpose(1, 0, 2)
        .reshape(128, NIG * NS))

    nc = build_nc()
    in_maps = []
    for core in range(8):
        in_maps.append({
            "xT": np.ascontiguousarray(x[core].T),
            "ipwT": ipwT, "convT": convT, "dtwT": dtwT, "bcwT": bcwT,
            "outwT": outwT, "w1T": w1T, "w2T": w2T, "b1r": b1r, "b2r": b2r,
            "convbr": convbr, "dtbr": dtbr, "Dr": Dr, "alogr": alogr,
            "maskT": np.ascontiguousarray(maskval[core].T),
        })
    res = run_bass_kernel_spmd(nc, in_maps, core_ids=list(range(8)))
    out = np.empty((8, SEQ, DM), np.float32)
    for core in range(8):
        out[core] = res.results[core]["outT"].T
    return out
